# version 07
# baseline (speedup 1.0000x reference)
import numpy as np
import concourse.bass as bass
import concourse.mybir as mybir
from concourse.bass_utils import run_bass_kernel_spmd

F32 = mybir.dt.float32
BF16 = mybir.dt.bfloat16
AF = mybir.ActivationFunctionType
ALU = mybir.AluOpType

NCORE = 8
S = 8192
D = 4096
NH = 32
DH = 128
OWN = 1024
HALO = 256
KRANGE = OWN + 2 * HALO
NKT = KRANGE // 128
NDC = D // 128
HF = 512
NEG = -30000.0
SCALE = 1.0 / float(np.sqrt(DH))
EPS = 1e-5

CS, SN, C2, S2, T1, VB, MB, RB = [i * 512 for i in range(8)]


def _gap(n):
    if n >= 384:
        return 160
    if n >= 192:
        return 80
    return 56


class P:
    ENGS = ("tensor", "scalar", "vector", "gpsimd", "sync")

    def __init__(self):
        self.q = {e: [] for e in self.ENGS}
        self.cnt = {s: 0 for s in ("sW", "sX", "sO", "sPE", "sACT", "sDVE", "sPOOL")}
        self.sems = {}
        self.pending = {}

    def wait(self, eng, sem, val):
        if val is None or val <= 0:
            return
        assert val <= self.cnt[sem], f"wait {sem}>={val}, recorded {self.cnt[sem]}"
        self.q[eng].append(lambda e, s=sem, v=val: e.wait_ge(self.sems[s], v))

    def op(self, eng, fn, inc=None, by=1):
        v = None
        if inc is not None:
            self.cnt[inc] += by
            v = self.cnt[inc]

        def g(e, fn=fn, inc=inc, by=by):
            ins = fn(e)
            if inc is not None:
                ins.then_inc(self.sems[inc], by)
        self.q[eng].append(g)
        return v

    def raw(self, eng, fn):
        self.q[eng].append(fn)

    def psum_read(self, eng, fn, inc):
        self.wait(eng, "sPE", self.cnt["sPE"])
        v = self.op(eng, fn, inc)
        self.pending[inc] = v
        return v

    def pe_barrier(self):
        for sem, v in self.pending.items():
            self.wait("tensor", sem, v)
        self.pending = {}





def _dbg_dump(p, t, extra=()):
    for eng in ("sync",):
        p.wait(eng, "sDVE", p.cnt["sDVE"])
        p.wait(eng, "sACT", p.cnt["sACT"])
        p.wait(eng, "sPE", p.cnt["sPE"])
    p.op("sync", lambda e: e.dma_start(out=t["d_out"][0], in_=t["xh"][:].rearrange("p (a b) -> p a b", b=HF)), "sO", 16)
    p.op("sync", lambda e: e.dma_start(out=t["d_dbg"][0], in_=t["krT"][:]), "sO", 16)
    p.op("sync", lambda e: e.dma_start(out=t["d_dbg"][1], in_=t["vrow"][:]), "sO", 16)
    p.op("sync", lambda e: e.dma_start(out=t["d_dbg2"], in_=t["ya"][:]), "sO", 16)
    p.op("sync", lambda e: e.dma_start(out=t["d_dbg3"], in_=t["qb"][:]), "sO", 16)
    p.wait("sync", "sO", p.cnt["sO"])


def build_program(p, t, stop=None):

    for name_d, name_s in [("ident_bf", "d_ident"), ("ones_bf", "d_ones"),
                           ("rot_bf", "d_rot"), ("e8_bf", "d_e8"),
                           ("at_bf", "d_at"), ("wr_bf", "d_wr"), ("bt_bf", "d_bt"),
                           ("maskone_bf", "d_maskone")]:
        p.op("gpsimd", lambda e, d=t[name_d], s=t[name_s]: e.dma_start(out=d[:], in_=s), "sW", 16)
    for name_d, name_s in [("aABf", "d_aABf"), ("aABn", "d_aABn"),
                           ("brt", "d_br"), ("ident_f", "d_identf")]:
        p.op("sync", lambda e, d=t[name_d], s=t[name_s]: e.dma_start(out=d[:], in_=s), "sX", 16)
    cW0, cX0 = p.cnt["sW"], p.cnt["sX"]
    for eng in ("tensor", "scalar", "vector"):
        p.wait(eng, "sW", cW0)
        p.wait(eng, "sX", cX0)

    pool = t["pool"]
    ya, qb, xh, pb = t["ya"], t["qb"], t["xh"], t["pbig"]

    def bank(i, w=512):
        return pb[:, i * 512:i * 512 + w]

    CSs, SNs, C2s, S2s, T1s, VBs, RBs = 0, 512, 1024, 1536, 2048, 2560, 3072

    st = {
        "ws_free": [("sPE", 0), ("sPE", 0)],
        "ws_ready": [("sW", 0), ("sW", 0)],
    }

    def wait_pair(eng, pair):
        p.wait(eng, pair[0], pair[1])

    def ln_block(xview_fn, W, ybuf, sa, sb, xdep=None):
        n = NDC * W
        full = xview_fn(None)
        if xdep:
            p.wait("vector", xdep[0], xdep[1])
        c_xbf = p.op("vector", lambda e: e.tensor_copy(sa[:, 0:n], full), "sDVE")
        if xdep:
            p.wait("scalar", xdep[0], xdep[1])
        c_xsq = p.op("scalar", lambda e: e.activation(sb[:, 0:n], full, AF.Square), "sACT")

        p.pe_barrier()
        p.wait("tensor", "sDVE", c_xbf)
        p.wait("tensor", "sACT", c_xsq)

        def mms(e):
            last = None
            for bi, src in ((0, sa), (1, sb)):
                for dc in range(NDC):
                    last = e.matmul(pb[:, bi * 512:bi * 512 + W], t["ones_bf"][:], src[:, dc * W:(dc + 1) * W],
                                    start=(dc == 0), stop=(dc == NDC - 1))
                    e.nop(cycle_cnt=_gap(W))
            return last
        p.op("tensor", mms, "sPE")

        p.psum_read("vector", lambda e: e.tensor_scalar_mul(pool[:, CSs:CSs + W], bank(0, W), 1.0 / D), "sDVE")
        p.psum_read("vector", lambda e: e.tensor_scalar(pool[:, VBs:VBs + W], bank(1, W), 1.0 / D, EPS, ALU.mult, ALU.add), "sDVE")
        p.op("vector", lambda e: e.tensor_mul(pool[:, T1s:T1s + W], pool[:, CSs:CSs + W], pool[:, CSs:CSs + W]), "sDVE")
        c_var = p.op("vector", lambda e: e.tensor_sub(pool[:, VBs:VBs + W], pool[:, VBs:VBs + W], pool[:, T1s:T1s + W]), "sDVE")
        p.wait("scalar", "sDVE", c_var)
        p.op("scalar", lambda e: e.activation(pool[:, T1s:T1s + W], pool[:, VBs:VBs + W], AF.Ln), "sACT")
        c_r = p.op("scalar", lambda e: e.activation(pool[:, RBs:RBs + W], pool[:, T1s:T1s + W], AF.Exp, scale=-0.5), "sACT")
        subs = []
        for dc in range(NDC):
            def sub_fn(e, dc=dc):
                return e.tensor_sub(ybuf[:, dc * W:(dc + 1) * W], xview_fn(dc), pool[:, CSs:CSs + W])
            subs.append(p.op("vector", sub_fn, "sDVE"))
        return c_r, subs

    def ffn_block(W, ybuf, xdst_fn, c_r, subs):
        p.pe_barrier()

        def mm_lg_low(e):
            for dc in range(NDC):
                if dc % 4 == 0:
                    e.wait_ge(p.sems["sDVE"], subs[min(dc + 3, NDC - 1)])
                e.matmul(pb[0:8, 2 * 512:2 * 512 + W], t["wr_bf"][:, dc * 8:(dc + 1) * 8],
                         ybuf[:, dc * W:(dc + 1) * W], start=(dc == 0), stop=(dc == NDC - 1))
                e.nop(cycle_cnt=_gap(W))
            last = None
            for dc in range(NDC):
                last = e.matmul(pb[:, 3 * 512:3 * 512 + W], t["at_bf"][:, dc * 128:(dc + 1) * 128],
                                ybuf[:, dc * W:(dc + 1) * W], start=(dc == 0), stop=(dc == NDC - 1))
                e.nop(cycle_cnt=_gap(W))
            return last
        p.op("tensor", mm_lg_low, "sPE")

        p.wait("vector", "sACT", c_r)
        c_l2 = p.psum_read("vector", lambda e: e.tensor_mul(pool[:, CSs:CSs + W], bank(2, W), pool[:, RBs:RBs + W]), "sDVE")
        p.wait("scalar", "sDVE", c_l2)
        c_exp = p.op("scalar", lambda e: e.activation(t["expb"][0:8, 0:W], pool[0:8, CSs:CSs + W], AF.Exp,
                                                      bias=t["brt"][0:8, :]), "sACT")

        p.pe_barrier()
        p.wait("tensor", "sACT", c_exp)

        def mm_e(e):
            e.matmul(bank(4, W), t["ones_bf"][0:8, :], t["expb"][0:8, 0:W], start=True, stop=True)
            e.nop(cycle_cnt=_gap(W))
            last = e.matmul(bank(5, W), t["e8_bf"][:], t["expb"][0:8, 0:W], start=True, stop=True)
            e.nop(cycle_cnt=_gap(W))
            return last
        p.op("tensor", mm_e, "sPE")

        p.psum_read("vector", lambda e: e.tensor_mul(pool[:, VBs:VBs + W], bank(5, W), pool[:, RBs:RBs + W]), "sDVE")
        p.psum_read("vector", lambda e: e.reciprocal(pool[:, T1s:T1s + W], bank(4, W)), "sDVE")
        p.op("vector", lambda e: e.tensor_mul(pool[:, VBs:VBs + W], pool[:, VBs:VBs + W], pool[:, T1s:T1s + W]), "sDVE")
        p.psum_read("vector", lambda e: e.tensor_mul(t["zbuf"][:, 0:W], bank(3, W), pool[:, VBs:VBs + W]), "sDVE")

        c_add = None
        for b0 in range(0, NDC, 4):
            p.pe_barrier()

            def mm_b(e, b0=b0, W=W):
                last = None
                for i in range(4):
                    last = e.matmul(pb[:, i * W:(i + 1) * W], t["bt_bf"][:, (b0 + i) * 128:(b0 + i + 1) * 128],
                                    t["zbuf"][:, 0:W], start=True, stop=True)
                    e.nop(cycle_cnt=_gap(W))
                return last
            p.op("tensor", mm_b, "sPE")

            def addf(e, b0=b0, W=W):
                dst = xdst_fn(b0)
                return e.tensor_add(dst, dst, pb[:, 0:4 * W])
            c_add = p.psum_read("vector", addf, "sDVE")
        return c_add

    def kv_block(u, c_r, c_y, kv_stop=None):
        W = 256
        ybuf = ya[:, 8192:8192 + NDC * W]
        p.wait("sync", "sDVE", c_y)
        cs1 = None
        for slot, name in ((CSs, "d_cos"), (SNs, "d_sin")):
            cs1 = p.op("sync", lambda e, slot=slot, name=name, u=u: e.dma_start(
                out=pool[:, slot:slot + W], in_=t[name][:, u * W:(u + 1) * W]), "sX", 16)
        p.wait("vector", "sX", cs1)
        p.wait("vector", "sACT", c_r)
        p.op("vector", lambda e: e.tensor_mul(pool[:, C2s:C2s + W], pool[:, CSs:CSs + W], pool[:, RBs:RBs + W]), "sDVE")
        c_cs = p.op("vector", lambda e: e.tensor_mul(pool[:, S2s:S2s + W], pool[:, SNs:SNs + W], pool[:, RBs:RBs + W]), "sDVE")

        if kv_stop == -1:
            return
        p.pe_barrier()
        p.wait("tensor", "sDVE", c_y)
        for w_i, wname in enumerate(("d_wk", "d_wv")):
            for half in range(2):
                bi = (2 * w_i + half) % 2
                wait_pair("gpsimd", st["ws_free"][bi])
                c_w = p.op("gpsimd", lambda e, bi=bi, wname=wname, half=half: e.dma_start(
                    out=t[f"ws{bi}"][:], in_=t[wname][:, half * 16:(half + 1) * 16, :]), "sW", 16)
                st["ws_ready"][bi] = ("sW", c_w)
                wait_pair("tensor", st["ws_ready"][bi])

                def mm_kv(e, bi=bi, half=half, w_i=w_i, W=W):
                    last = None
                    for i in range(16):
                        dc = half * 16 + i
                        last = e.matmul(pb[:, w_i * 512:w_i * 512 + W], t[f"ws{bi}"][:, i * 128:(i + 1) * 128],
                                        ybuf[:, dc * W:(dc + 1) * W],
                                        start=(dc == 0), stop=(dc == NDC - 1))
                        e.nop(cycle_cnt=_gap(W))
                    return last
                c_pe = p.op("tensor", mm_kv, "sPE")
                st["ws_free"][bi] = ("sPE", c_pe)

        if kv_stop == 0:
            return
        c_kraw = p.psum_read("scalar", lambda e, W=W: e.copy(t["qk_raw"][:, 0:W], bank(0, W)), "sACT")
        if kv_stop == 11:
            return
        c_vtf = p.psum_read("vector", lambda e, W=W: e.tensor_mul(pool[:, T1s:T1s + W], bank(1, W), pool[:, RBs:RBs + W]), "sDVE")
        if kv_stop == 1:
            return

        p.pe_barrier()
        p.wait("tensor", "sACT", c_kraw)
        p.wait("tensor", "sDVE", c_vtf)

        def mm_rot_tr(e, W=W):
            e.matmul(bank(2, W), t["rot_bf"][:], t["qk_raw"][:, 0:W], start=True, stop=True)
            e.nop(cycle_cnt=_gap(W))
            last = None
            for b in range(2):
                last = e.transpose(pb[:, 3 * 512 + b * 128:3 * 512 + (b + 1) * 128],
                                   pool[:, T1s + b * 128:T1s + (b + 1) * 128], t["ident_f"][:])
                e.nop(cycle_cnt=160)
            return last
        p.op("tensor", mm_rot_tr, "sPE")
        if kv_stop == 2:
            return

        p.wait("vector", "sDVE", c_cs)
        p.wait("vector", "sACT", c_kraw)
        p.op("vector", lambda e, W=W: e.tensor_mul(pool[:, SNs:SNs + W], t["qk_raw"][:, 0:W], pool[:, C2s:C2s + W]), "sDVE")
        p.psum_read("vector", lambda e, W=W: e.tensor_mul(pool[:, VBs:VBs + W], bank(2, W), pool[:, S2s:S2s + W]), "sDVE")
        p.op("vector", lambda e, u=u, W=W: e.tensor_add(t["krT"][:, u * W:(u + 1) * W],
                                                        pool[:, SNs:SNs + W], pool[:, VBs:VBs + W]), "sDVE")
        if kv_stop == 3:
            return
        p.psum_read("scalar", lambda e, u=u: e.copy(t["vrow"][:, 2 * u * 128:2 * u * 128 + 256],
                                                    pb[:, 3 * 512:3 * 512 + 256]), "sACT")

    xup = xh[:, 0:NDC * 256]

    def xup_fn(dc, W=256):
        if dc is None:
            return xup
        return xh[:, dc * W:(dc + 1) * W]

    for u in range(6):
        p.wait("sync", "sDVE", p.cnt["sDVE"])
        p.wait("sync", "sACT", p.cnt["sACT"])
        c_x = p.op("sync", lambda e, u=u: e.dma_start(
            out=xup.rearrange("p (a b) -> p a b", b=256), in_=t["d_xt"][u]), "sX", 16)
        c_r, subs1 = ln_block(xup_fn, 256, ya[:, 8192:16384], ya, qb, xdep=("sX", c_x))
        if stop is not None and stop[0] == "upln" and u == stop[1] - 1:
            _dbg_dump(p, t)
            return
        c_add = ffn_block(256, ya[:, 8192:16384], lambda oc: xh[:, oc * 256:(oc + 4) * 256], c_r, subs1)
        if stop is not None and stop[0] == "upffn" and u == stop[1] - 1:
            _dbg_dump(p, t)
            return
        c_r2, subs2 = ln_block(xup_fn, 256, ya[:, 8192:16384], ya, qb, xdep=("sDVE", c_add))
        if stop is not None and stop[0] == "upln2" and u == stop[1] - 1:
            _dbg_dump(p, t)
            return
        kv_block(u, c_r2, subs2[NDC - 1],
                 kv_stop=(stop[2] if stop is not None and stop[0] == "kv" and u == stop[1] - 1 else None))
        if stop is not None and stop[0] == "kv" and u == stop[1] - 1:
            _dbg_dump(p, t)
            return
        if stop is not None and stop[0] == "up" and u == stop[1] - 1:
            _dbg_dump(p, t)
            return

    for h in range(2):
        def half_dst(k):
            return xh[:].rearrange("p (a b) -> p a b", b=HF)[:, :, k * 256:(k + 1) * 256]

        p.wait("sync", "sDVE", p.cnt["sDVE"])
        p.wait("sync", "sACT", p.cnt["sACT"])
        c_x2 = None
        for k in range(2):
            c_x2 = p.op("sync", lambda e, h=h, k=k: e.dma_start(
                out=half_dst(k), in_=t["d_xt"][1 + 2 * h + k]), "sX", 16)

        def xhf_fn(dc, W=HF):
            if dc is None:
                return xh[:, 0:NDC * W]
            return xh[:, dc * W:(dc + 1) * W]

        c_r, subs1 = ln_block(xhf_fn, HF, ya[:, 0:NDC * HF], ya, qb, xdep=("sX", c_x2))
        c_add = ffn_block(HF, ya[:, 0:NDC * HF], lambda oc: xh[:, oc * HF:(oc + 4) * HF], c_r, subs1)
        c_r2, subs2 = ln_block(xhf_fn, HF, ya[:, 0:NDC * HF], ya, qb, xdep=("sDVE", c_add))
        if stop is not None and stop[0] == "half" and stop[1] == "ffn1":
            _dbg_dump(p, t)
            return

        q0 = HALO + h * HF
        p.wait("sync", "sDVE", subs2[NDC - 1])
        cs1 = None
        for slot, name in ((CSs, "d_cos"), (SNs, "d_sin")):
            cs1 = p.op("sync", lambda e, slot=slot, name=name, q0=q0: e.dma_start(
                out=pool[:, slot:slot + HF], in_=t[name][:, q0:q0 + HF]), "sX", 16)
        p.wait("vector", "sX", cs1)
        p.wait("vector", "sACT", c_r2)
        p.op("vector", lambda e: e.tensor_mul(pool[:, C2s:C2s + HF], pool[:, CSs:CSs + HF], pool[:, RBs:RBs + HF]), "sDVE")
        c_cs = p.op("vector", lambda e: e.tensor_mul(pool[:, S2s:S2s + HF], pool[:, SNs:SNs + HF], pool[:, RBs:RBs + HF]), "sDVE")

        for hd in range(NH):
            par = hd % 2
            for half in range(2):
                bi = half
                wait_pair("gpsimd", st["ws_free"][bi])
                c_w = p.op("gpsimd", lambda e, bi=bi, hd=hd, half=half: e.dma_start(
                    out=t[f"ws{bi}"][:], in_=t["d_wq"][hd, :, half * 16:(half + 1) * 16, :]), "sW", 16)
                st["ws_ready"][bi] = ("sW", c_w)
            p.pe_barrier()
            if hd == 0:
                p.wait("tensor", "sDVE", subs2[NDC - 1])
            for half in range(2):
                bi = half
                wait_pair("tensor", st["ws_ready"][bi])

                def mm_q(e, bi=bi, half=half, par=par):
                    last = None
                    for i in range(16):
                        dc = half * 16 + i
                        last = e.matmul(bank(par), t[f"ws{bi}"][:, i * 128:(i + 1) * 128],
                                        ya[:, dc * HF:(dc + 1) * HF],
                                        start=(dc == 0), stop=(dc == NDC - 1))
                        e.nop(cycle_cnt=_gap(HF))
                    return last
                c_pe = p.op("tensor", mm_q, "sPE")
                st["ws_free"][bi] = ("sPE", c_pe)
            qr_off = par * HF
            c_qraw = p.psum_read("scalar", lambda e, qr_off=qr_off, par=par: e.copy(
                t["qk_raw"][:, qr_off:qr_off + HF], bank(par)), "sACT")
            p.pe_barrier()
            p.wait("tensor", "sACT", c_qraw)

            def mm_rot(e, qr_off=qr_off, par=par):
                last = e.matmul(bank(2 + par), t["rot_bf"][:], t["qk_raw"][:, qr_off:qr_off + HF], start=True, stop=True)
                e.nop(cycle_cnt=_gap(HF))
                return last
            p.op("tensor", mm_rot, "sPE")
            if hd == 0:
                p.wait("vector", "sDVE", c_cs)
            p.wait("vector", "sACT", c_qraw)
            tsl = T1s if par == 0 else VBs
            msl = CSs if par == 0 else SNs
            p.op("vector", lambda e, qr_off=qr_off, tsl=tsl: e.tensor_mul(
                pool[:, tsl:tsl + HF], t["qk_raw"][:, qr_off:qr_off + HF], pool[:, C2s:C2s + HF]), "sDVE")
            p.psum_read("vector", lambda e, msl=msl, par=par: e.tensor_mul(
                pool[:, msl:msl + HF], bank(2 + par), pool[:, S2s:S2s + HF]), "sDVE")
            p.op("vector", lambda e, hd=hd, tsl=tsl, msl=msl: e.tensor_add(
                qb[:, hd * HF:(hd + 1) * HF], pool[:, tsl:tsl + HF], pool[:, msl:msl + HF]), "sDVE")
        if stop is not None and stop[0] == "half" and stop[1] == "q":
            _dbg_dump(p, t)
            return

        c_q_done = p.cnt["sDVE"]
        p.wait("tensor", "sDVE", c_q_done)
        for j2 in range(2):
            j = 2 * h + j2
            aAB = t["aABf"] if j == 0 else t["aABn"]
            sig_c = [None, None]
            for k in range(NH + 2):
                p.pe_barrier()
                if k < NH:
                    par = k % 2
                    qsl = qb[:, k * HF + j2 * 256: k * HF + j2 * 256 + 256]

                    def mm_s(e, j=j, par=par, qsl=qsl):
                        last = None
                        for ti in range(6):
                            tile = 2 * j + ti
                            last = e.matmul(pb[:, par * 1536 + ti * 256: par * 1536 + (ti + 1) * 256],
                                            t["krT"][:, tile * 128:(tile + 1) * 128],
                                            qsl, start=True, stop=True)
                            e.nop(cycle_cnt=_gap(256))
                        return last
                    p.op("tensor", mm_s, "sPE")
                if 0 < k <= NH:
                    km = k - 1
                    parm = km % 2
                    sgm = t["sig0"] if parm == 0 else t["sig1"]
                    p.wait("tensor", "sACT", sig_c[parm])

                    def mm_sw(e, j=j, sgm=sgm):
                        for w_i in range(2):
                            for i in range(4):
                                ti = w_i * 2 + i
                                tile = 2 * j + ti
                                e.matmul(pb[:, 6 * 512 + w_i * 256:6 * 512 + (w_i + 1) * 256],
                                         t["maskone_bf"][:, tile * 128:(tile + 1) * 128],
                                         sgm[:, ti * 256:(ti + 1) * 256],
                                         start=(i == 0), stop=(i == 3))
                                e.nop(cycle_cnt=_gap(256))
                        last = None
                        for w_i in range(2):
                            for i in range(4):
                                ti = w_i * 2 + i
                                tile = 2 * j + ti
                                last = e.matmul(pb[:, 7 * 512 + w_i * 256:7 * 512 + (w_i + 1) * 256],
                                                t["vrow"][:, tile * 128:(tile + 1) * 128],
                                                sgm[:, ti * 256:(ti + 1) * 256],
                                                start=(i == 0), stop=(i == 3))
                                e.nop(cycle_cnt=_gap(256))
                        return last
                    p.op("tensor", mm_sw, "sPE")
                if k < NH:
                    par = k % 2
                    sg = t["sig0"] if par == 0 else t["sig1"]
                    sig_c[par] = p.psum_read("scalar", lambda e, par=par, sg=sg: e.activation(
                        sg[:, 0:1536], pb[:, par * 1536:par * 1536 + 1536], AF.Sigmoid, scale=SCALE), "sACT")
                if 0 < k <= NH:
                    km = k - 1
                    pm = km % 2
                    ssl = (S2s, C2s)[pm]
                    vsl = (VBs, SNs)[pm]
                    c_se = p.psum_read("scalar", lambda e, ssl=ssl: e.copy(pool[:, ssl:ssl + 512], bank(6)), "sACT")
                    p.psum_read("vector", lambda e, aAB=aAB, vsl=vsl: e.tensor_mul(pool[:, vsl:vsl + 512], bank(7), aAB[:]), "sDVE")
                    st["att_se%d" % pm] = c_se
                if k >= 2:
                    km1 = k - 2
                    pm1 = km1 % 2
                    ssl1 = (S2s, C2s)[pm1]
                    vsl1 = (VBs, SNs)[pm1]
                    tsl1 = (T1s, CSs)[pm1]
                    p.wait("vector", "sACT", st["att_se%d" % pm1])
                    p.op("vector", lambda e, ssl1=ssl1, tsl1=tsl1: e.reciprocal(pool[:, tsl1:tsl1 + 512], pool[:, ssl1:ssl1 + 512]), "sDVE")
                    p.op("vector", lambda e, vsl1=vsl1, tsl1=tsl1: e.tensor_mul(pool[:, vsl1:vsl1 + 512], pool[:, vsl1:vsl1 + 512], pool[:, tsl1:tsl1 + 512]), "sDVE")
                    p.op("vector", lambda e, km1=km1, j2=j2, vsl1=vsl1: e.tensor_add(
                        ya[:, km1 * HF + j2 * 256: km1 * HF + j2 * 256 + 256],
                        pool[:, vsl1:vsl1 + 256], pool[:, vsl1 + 256:vsl1 + 512]), "sDVE")
        if stop is not None and stop[0] == "half" and stop[1] == "attn":
            _dbg_dump(p, t)
            return

        c_attn_done = p.cnt["sDVE"]
        p.wait("tensor", "sDVE", c_attn_done)
        for oc in range(NDC):
            par = oc % 2
            for half in range(2):
                bi = half
                wait_pair("gpsimd", st["ws_free"][bi])
                c_w = p.op("gpsimd", lambda e, bi=bi, oc=oc, half=half: e.dma_start(
                    out=t[f"ws{bi}"][:], in_=t["d_wo"][oc, :, half * 16:(half + 1) * 16, :]), "sW", 16)
                st["ws_ready"][bi] = ("sW", c_w)
            if par == 0:
                p.pe_barrier()
            for half in range(2):
                bi = half
                wait_pair("tensor", st["ws_ready"][bi])

                def mm_o(e, bi=bi, half=half, par=par):
                    last = None
                    for i in range(16):
                        dc = half * 16 + i
                        last = e.matmul(bank(par), t[f"ws{bi}"][:, i * 128:(i + 1) * 128],
                                        ya[:, dc * HF:(dc + 1) * HF],
                                        start=(dc == 0), stop=(dc == NDC - 1))
                        e.nop(cycle_cnt=_gap(HF))
                    return last
                c_pe = p.op("tensor", mm_o, "sPE")
                st["ws_free"][bi] = ("sPE", c_pe)
            if par == 1:
                def addo(e, oc=oc):
                    dst = xh[:, (oc - 1) * HF:(oc + 1) * HF]
                    return e.tensor_add(dst, dst, pb[:, 0:1024])
                p.psum_read("vector", addo, "sDVE")
        if stop is not None and stop[0] == "half" and stop[1] == "wo":
            _dbg_dump(p, t)
            return

        c_r3, subs3 = ln_block(xhf_fn, HF, ya[:, 0:NDC * HF], ya, qb,
                               xdep=("sDVE", p.cnt["sDVE"]))
        c_out = ffn_block(HF, ya[:, 0:NDC * HF], lambda oc: xh[:, oc * HF:(oc + 4) * HF], c_r3, subs3)

        p.wait("sync", "sDVE", c_out)
        c_st = p.op("sync", lambda e, h=h: e.dma_start(
            out=t["d_out"][h], in_=xh[:].rearrange("p (a b) -> p a b", b=HF)), "sO", 16)
        if h == 0:
            p.wait("sync", "sO", c_st)

    p.wait("sync", "sO", p.cnt["sO"])


def _rope_tables():
    Dh = DH
    inv_freq = 1.0 / 10000.0 ** (np.arange(0, Dh, 2, dtype=np.float32) / Dh)
    tt = np.arange(S, dtype=np.float32)
    f = np.outer(tt, inv_freq)
    cos = np.zeros((S, Dh), dtype=np.float32)
    sin = np.zeros((S, Dh), dtype=np.float32)
    cos[:, 0::2] = np.cos(f)
    cos[:, 1::2] = np.cos(f)
    sin[:, 0::2] = np.sin(f)
    sin[:, 1::2] = np.sin(f)
    return cos, sin


def _declare(nc):
    t = {}

    def dram(name, shape, dt=F32):
        t["d_" + name] = nc.dram_tensor(name, list(shape), dt, kind="ExternalInput").ap()

    dram("xt", (6, 128, NDC, 256))
    dram("wq", (NH, 128, NDC, 128))
    dram("wo", (NDC, 128, NDC, 128))
    dram("wk", (128, NDC, 128))
    dram("wv", (128, NDC, 128))
    dram("at", (128, NDC * 128))
    dram("wr", (128, NDC * 8))
    dram("bt", (128, D))
    dram("br", (8, 1))
    dram("cos", (128, KRANGE))
    dram("sin", (128, KRANGE))
    dram("ident", (128, 128))
    dram("ones", (128, 128))
    dram("rot", (128, 128))
    dram("e8", (8, 128))
    dram("identf", (128, 128))
    dram("aABf", (128, 512))
    dram("aABn", (128, 512))
    dram("maskone", (128, NKT * 128))
    t["d_out"] = nc.dram_tensor("out", [2, 128, NDC, HF], F32, kind="ExternalOutput").ap()
    t["d_dbg"] = nc.dram_tensor("dbg", [2, 128, KRANGE], BF16, kind="ExternalOutput").ap()
    t["d_dbg2"] = nc.dram_tensor("dbg2", [128, NDC * HF], BF16, kind="ExternalOutput").ap()
    t["d_dbg3"] = nc.dram_tensor("dbg3", [128, NDC * HF], BF16, kind="ExternalOutput").ap()
    return t


def _alloc(nc, t, stack):
    def sb(name, shape, dt):
        t[name] = stack.enter_context(nc.sbuf_tensor("sb_" + name, list(shape), dt))

    sb("ident_bf", (128, 128), BF16)
    sb("ones_bf", (128, 128), BF16)
    sb("rot_bf", (128, 128), BF16)
    sb("e8_bf", (8, 128), BF16)
    sb("ident_f", (128, 128), F32)
    sb("aABf", (128, 512), F32)
    sb("aABn", (128, 512), F32)
    sb("maskone_bf", (128, NKT * 128), BF16)
    sb("brt", (8, 1), F32)
    sb("at_bf", (128, NDC * 128), BF16)
    sb("wr_bf", (128, NDC * 8), BF16)
    sb("bt_bf", (128, D), BF16)
    sb("krT", (128, KRANGE), BF16)
    sb("vrow", (128, KRANGE), BF16)
    sb("xh", (128, NDC * HF), F32)
    sb("ya", (128, NDC * HF), BF16)
    sb("qb", (128, NDC * HF), BF16)
    sb("ws0", (128, 2048), BF16)
    sb("ws1", (128, 2048), BF16)
    sb("qk_raw", (128, 1024), BF16)
    sb("zbuf", (128, HF), BF16)
    sb("sig0", (128, 6 * 256), BF16)
    sb("sig1", (128, 6 * 256), BF16)
    sb("expb", (8, HF), BF16)
    sb("pool", (128, 7 * 512), F32)
    t["pbig"] = stack.enter_context(nc.psum_tensor("pbig", [128, 4096], F32))


def _host_inputs(L0_attn_x, L0_attn_Wq, L0_attn_Wk, L0_attn_Wv, L0_attn_Wo,
                 L0_ffn_A, L0_ffn_B, L0_ffn_Wr, L0_ffn_br):
    x = np.asarray(L0_attn_x, dtype=np.float32)[0]
    Wq = np.ascontiguousarray(np.asarray(L0_attn_Wq, np.float32))
    Wk = np.asarray(L0_attn_Wk, np.float32)
    Wv = np.asarray(L0_attn_Wv, np.float32)
    Wo = np.ascontiguousarray(np.asarray(L0_attn_Wo, np.float32))
    A = np.asarray(L0_ffn_A, np.float32)
    B = np.asarray(L0_ffn_B, np.float32)
    Wr = np.asarray(L0_ffn_Wr, np.float32)
    br = np.asarray(L0_ffn_br, np.float32)

    xT = np.ascontiguousarray(x.T)
    cos, sin = _rope_tables()
    cosT = np.ascontiguousarray(cos.T)
    sinT = np.ascontiguousarray(sin.T)

    wq_l = np.ascontiguousarray(Wq.reshape(NH, 128, NDC, 128).transpose(0, 3, 2, 1))
    wo_l = np.ascontiguousarray(Wo.reshape(NDC, 128, NDC, 128).transpose(0, 3, 2, 1))
    wk_l = np.ascontiguousarray(Wk.reshape(128, NDC, 128).transpose(2, 1, 0))
    wv_l = np.ascontiguousarray(Wv.reshape(128, NDC, 128).transpose(2, 1, 0))
    at_l = np.ascontiguousarray(A.reshape(128, NDC, 128).transpose(2, 1, 0)).reshape(128, NDC * 128)
    wr_l = np.ascontiguousarray(Wr.reshape(8, NDC, 128).transpose(2, 1, 0)).reshape(128, NDC * 8)
    bt_l = np.ascontiguousarray(B.transpose(1, 0, 2).reshape(D, 128).T)
    br_l = br.reshape(8, 1)

    ident = np.eye(128, dtype=np.float32)
    ones = np.ones((128, 128), dtype=np.float32)
    rot = np.zeros((128, 128), dtype=np.float32)
    for i in range(64):
        rot[i + 64, i] = -1.0
        rot[i, i + 64] = 1.0
    e8 = np.zeros((8, 128), dtype=np.float32)
    for hh in range(8):
        e8[hh, hh * 16:(hh + 1) * 16] = 8.0
    alpha = np.linspace(0.0, 1.0, 256, dtype=np.float32)
    aABn = np.zeros((128, 512), dtype=np.float32)
    aABn[:, 0:256] = 1.0 - alpha[None, :]
    aABn[:, 256:512] = alpha[None, :]

    shared = {
        "wq": wq_l, "wo": wo_l, "wk": wk_l, "wv": wv_l,
        "at": at_l, "wr": wr_l, "bt": bt_l, "br": br_l,
        "ident": ident, "identf": ident, "ones": ones, "rot": rot, "e8": e8,
        "aABn": aABn,
    }

    in_maps = []
    for c in range(NCORE):
        g0 = c * OWN - HALO
        xt_c = np.zeros((D, KRANGE), dtype=np.float32)
        lo, hi = max(0, g0), min(S, g0 + KRANGE)
        xt_c[:, lo - g0:hi - g0] = xT[:, lo:hi]
        xt_l = np.ascontiguousarray(
            xt_c.reshape(NDC, 128, 6, 256).transpose(2, 1, 0, 3))
        cos_c = np.zeros((DH, KRANGE), dtype=np.float32)
        sin_c = np.zeros((DH, KRANGE), dtype=np.float32)
        cos_c[:, lo - g0:hi - g0] = cosT[:, lo:hi]
        sin_c[:, lo - g0:hi - g0] = sinT[:, lo:hi]
        mo = np.zeros((128, NKT * 128), dtype=np.float32)
        for tt in range(NKT):
            grow = g0 + tt * 128 + np.arange(128)
            valid = ((grow >= 0) & (grow < S)).astype(np.float32)
            mo[:, tt * 128:(tt + 1) * 128] = valid[:, None]
        if c == 0:
            aABf = np.zeros((128, 512), dtype=np.float32)
            aABf[:, 256:512] = 1.0
        else:
            aABf = aABn
        m = dict(shared)
        m.update({"xt": xt_l, "cos": np.ascontiguousarray(cos_c),
                  "sin": np.ascontiguousarray(sin_c), "maskone": mo,
                  "aABf": np.ascontiguousarray(aABf)})
        in_maps.append(m)
    return in_maps


def kernel(L0_attn_x, L0_attn_Wq, L0_attn_Wk, L0_attn_Wv, L0_attn_Wo,
           L0_ffn_A, L0_ffn_B, L0_ffn_Wr, L0_ffn_br):
    from contextlib import ExitStack

    in_maps = _host_inputs(L0_attn_x, L0_attn_Wq, L0_attn_Wk, L0_attn_Wv,
                           L0_attn_Wo, L0_ffn_A, L0_ffn_B, L0_ffn_Wr, L0_ffn_br)

    nc = bass.Bass("TRN2", target_bir_lowering=False)
    t = _declare(nc)
    p = P()
    with ExitStack() as stack:
        _alloc(nc, t, stack)
        sems = {}
        for s in ("sW", "sX", "sO", "sPE", "sACT", "sDVE", "sPOOL"):
            sems[s] = stack.enter_context(nc.semaphore(s))
        p.sems = sems
        build_program(p, t)
        blk = stack.enter_context(nc.Block())
        for eng_name in P.ENGS:
            closures = p.q[eng_name]

            def make_body(closures):
                def body(eng):
                    for fn in closures:
                        fn(eng)
                return body
            getattr(blk, eng_name)(make_body(closures))

    res = run_bass_kernel_spmd(nc, in_maps, core_ids=list(range(NCORE)))
    outs = []
    for c in range(NCORE):
        o = res.results[c]["out"]
        outs.append(o.transpose(2, 1, 0, 3).reshape(D, OWN))
    full = np.concatenate(outs, axis=1)
    return np.ascontiguousarray(full.T).reshape(1, S, D).astype(np.float32)


# version 08
# speedup vs baseline: 1.1493x; 1.1493x over previous
import numpy as np
import concourse.bass as bass
import concourse.mybir as mybir
from concourse.bass_utils import run_bass_kernel_spmd

F32 = mybir.dt.float32
BF16 = mybir.dt.bfloat16
AF = mybir.ActivationFunctionType
ALU = mybir.AluOpType

NCORE = 8
S = 8192
D = 4096
NH = 32
DH = 128
OWN = 1024
HALO = 256
KRANGE = OWN + 2 * HALO
NKT = KRANGE // 128
NDC = D // 128
HF = 512
NEG = -30000.0
SCALE = 1.0 / float(np.sqrt(DH))
EPS = 1e-5

CS, SN, C2, S2, T1, VB, MB, RB = [i * 512 for i in range(8)]


def _gap(n):
    if n >= 384:
        return 160
    if n >= 192:
        return 80
    return 56


class P:
    ENGS = ("tensor", "scalar", "vector", "gpsimd", "sync")

    def __init__(self):
        self.q = {e: [] for e in self.ENGS}
        self.cnt = {s: 0 for s in ("sW", "sX", "sO", "sPE", "sACT", "sDVE", "sPOOL")}
        self.sems = {}
        self.pending = {}
        self.keepwarm = None

    def wait(self, eng, sem, val):
        if val is None or val <= 0:
            return
        assert val <= self.cnt[sem], f"wait {sem}>={val}, recorded {self.cnt[sem]}"
        self.q[eng].append(lambda e, s=sem, v=val: e.wait_ge(self.sems[s], v))

    def op(self, eng, fn, inc=None, by=1):
        v = None
        if inc is not None:
            self.cnt[inc] += by
            v = self.cnt[inc]

        def g(e, fn=fn, inc=inc, by=by):
            ins = fn(e)
            if inc is not None:
                ins.then_inc(self.sems[inc], by)
        self.q[eng].append(g)
        return v

    def raw(self, eng, fn):
        self.q[eng].append(fn)

    def psum_read(self, eng, fn, inc):
        self.wait(eng, "sPE", self.cnt["sPE"])
        v = self.op(eng, fn, inc)
        self.pending[inc] = v
        return v

    def pe_barrier(self):
        if self.pending and self.keepwarm is not None:
            kw = self.keepwarm

            def dummies(e, kw=kw):
                for _ in range(16):
                    e.ldweights(kw[:])
            self.raw("tensor", dummies)
        for sem, v in self.pending.items():
            self.wait("tensor", sem, v)
        self.pending = {}





def _dbg_dump(p, t, extra=()):
    for eng in ("sync",):
        p.wait(eng, "sDVE", p.cnt["sDVE"])
        p.wait(eng, "sACT", p.cnt["sACT"])
        p.wait(eng, "sPE", p.cnt["sPE"])
    p.op("sync", lambda e: e.dma_start(out=t["d_out"][0], in_=t["xh"][:].rearrange("p (a b) -> p a b", b=HF)), "sO", 16)
    p.op("sync", lambda e: e.dma_start(out=t["d_dbg"][0], in_=t["krT"][:]), "sO", 16)
    p.op("sync", lambda e: e.dma_start(out=t["d_dbg"][1], in_=t["vrow"][:]), "sO", 16)
    p.op("sync", lambda e: e.dma_start(out=t["d_dbg2"], in_=t["ya"][:]), "sO", 16)
    p.op("sync", lambda e: e.dma_start(out=t["d_dbg3"], in_=t["qb"][:]), "sO", 16)
    p.wait("sync", "sO", p.cnt["sO"])


def build_program(p, t, stop=None):

    for name_d, name_s in [("ident_bf", "d_ident"), ("ones_bf", "d_ones"),
                           ("rot_bf", "d_rot"), ("e8_bf", "d_e8"),
                           ("at_bf", "d_at"), ("wr_bf", "d_wr"), ("bt_bf", "d_bt"),
                           ("maskone_bf", "d_maskone")]:
        p.op("gpsimd", lambda e, d=t[name_d], s=t[name_s]: e.dma_start(out=d[:], in_=s), "sW", 16)
    for name_d, name_s in [("aABf", "d_aABf"), ("aABn", "d_aABn"),
                           ("brt", "d_br"), ("ident_f", "d_identf")]:
        p.op("sync", lambda e, d=t[name_d], s=t[name_s]: e.dma_start(out=d[:], in_=s), "sX", 16)
    cW0, cX0 = p.cnt["sW"], p.cnt["sX"]
    for eng in ("tensor", "scalar", "vector"):
        p.wait(eng, "sW", cW0)
        p.wait(eng, "sX", cX0)

    pool = t["pool"]
    ya, qb, xh, pb = t["ya"], t["qb"], t["xh"], t["pbig"]
    p.keepwarm = t["ident_bf"]

    def bank(i, w=512):
        return pb[:, i * 512:i * 512 + w]

    CSs, SNs, C2s, S2s, T1s, VBs, RBs = 0, 512, 1024, 1536, 2048, 2560, 3072

    st = {
        "ws_free": [("sPE", 0), ("sPE", 0)],
        "ws_ready": [("sW", 0), ("sW", 0)],
    }

    def wait_pair(eng, pair):
        p.wait(eng, pair[0], pair[1])

    def ln_block(xview_fn, W, ybuf, sa, sb, xdep=None):
        n = NDC * W
        full = xview_fn(None)
        if xdep:
            p.wait("vector", xdep[0], xdep[1])
        c_xbf = p.op("vector", lambda e: e.tensor_copy(sa[:, 0:n], full), "sDVE")
        if xdep:
            p.wait("scalar", xdep[0], xdep[1])
        c_xsq = p.op("scalar", lambda e: e.activation(sb[:, 0:n], full, AF.Square), "sACT")

        p.pe_barrier()
        p.wait("tensor", "sDVE", c_xbf)
        p.wait("tensor", "sACT", c_xsq)

        def mms(e):
            last = None
            for bi, src in ((0, sa), (1, sb)):
                for dc in range(NDC):
                    last = e.matmul(pb[:, bi * 512:bi * 512 + W], t["ones_bf"][:], src[:, dc * W:(dc + 1) * W],
                                    start=(dc == 0), stop=(dc == NDC - 1))
                    e.nop(cycle_cnt=_gap(W))
            return last
        p.op("tensor", mms, "sPE")

        p.psum_read("vector", lambda e: e.tensor_scalar_mul(pool[:, CSs:CSs + W], bank(0, W), 1.0 / D), "sDVE")
        p.psum_read("vector", lambda e: e.tensor_scalar(pool[:, VBs:VBs + W], bank(1, W), 1.0 / D, EPS, ALU.mult, ALU.add), "sDVE")
        p.op("vector", lambda e: e.tensor_mul(pool[:, T1s:T1s + W], pool[:, CSs:CSs + W], pool[:, CSs:CSs + W]), "sDVE")
        c_var = p.op("vector", lambda e: e.tensor_sub(pool[:, VBs:VBs + W], pool[:, VBs:VBs + W], pool[:, T1s:T1s + W]), "sDVE")
        p.wait("scalar", "sDVE", c_var)
        p.op("scalar", lambda e: e.activation(pool[:, T1s:T1s + W], pool[:, VBs:VBs + W], AF.Ln), "sACT")
        c_r = p.op("scalar", lambda e: e.activation(pool[:, RBs:RBs + W], pool[:, T1s:T1s + W], AF.Exp, scale=-0.5), "sACT")
        subs = []
        for dc in range(NDC):
            def sub_fn(e, dc=dc):
                return e.tensor_sub(ybuf[:, dc * W:(dc + 1) * W], xview_fn(dc), pool[:, CSs:CSs + W])
            subs.append(p.op("vector", sub_fn, "sDVE"))
        return c_r, subs

    def ffn_block(W, ybuf, xdst_fn, c_r, subs):
        p.pe_barrier()

        def mm_lg_low(e):
            for dc in range(NDC):
                if dc % 4 == 0:
                    e.wait_ge(p.sems["sDVE"], subs[min(dc + 3, NDC - 1)])
                e.matmul(pb[0:8, 2 * 512:2 * 512 + W], t["wr_bf"][:, dc * 8:(dc + 1) * 8],
                         ybuf[:, dc * W:(dc + 1) * W], start=(dc == 0), stop=(dc == NDC - 1))
                e.nop(cycle_cnt=_gap(W))
            last = None
            for dc in range(NDC):
                last = e.matmul(pb[:, 3 * 512:3 * 512 + W], t["at_bf"][:, dc * 128:(dc + 1) * 128],
                                ybuf[:, dc * W:(dc + 1) * W], start=(dc == 0), stop=(dc == NDC - 1))
                e.nop(cycle_cnt=_gap(W))
            return last
        p.op("tensor", mm_lg_low, "sPE")

        p.wait("vector", "sACT", c_r)
        c_l2 = p.psum_read("vector", lambda e: e.tensor_mul(pool[:, CSs:CSs + W], bank(2, W), pool[:, RBs:RBs + W]), "sDVE")
        p.wait("scalar", "sDVE", c_l2)
        c_exp = p.op("scalar", lambda e: e.activation(t["expb"][0:8, 0:W], pool[0:8, CSs:CSs + W], AF.Exp,
                                                      bias=t["brt"][0:8, :]), "sACT")

        p.pe_barrier()
        p.wait("tensor", "sACT", c_exp)

        def mm_e(e):
            e.matmul(bank(4, W), t["ones_bf"][0:8, :], t["expb"][0:8, 0:W], start=True, stop=True)
            e.nop(cycle_cnt=_gap(W))
            last = e.matmul(bank(5, W), t["e8_bf"][:], t["expb"][0:8, 0:W], start=True, stop=True)
            e.nop(cycle_cnt=_gap(W))
            return last
        p.op("tensor", mm_e, "sPE")

        p.psum_read("vector", lambda e: e.tensor_mul(pool[:, VBs:VBs + W], bank(5, W), pool[:, RBs:RBs + W]), "sDVE")
        p.psum_read("vector", lambda e: e.reciprocal(pool[:, T1s:T1s + W], bank(4, W)), "sDVE")
        p.op("vector", lambda e: e.tensor_mul(pool[:, VBs:VBs + W], pool[:, VBs:VBs + W], pool[:, T1s:T1s + W]), "sDVE")
        p.psum_read("vector", lambda e: e.tensor_mul(t["zbuf"][:, 0:W], bank(3, W), pool[:, VBs:VBs + W]), "sDVE")

        c_add = None
        for b0 in range(0, NDC, 4):
            p.pe_barrier()

            def mm_b(e, b0=b0, W=W):
                last = None
                for i in range(4):
                    last = e.matmul(pb[:, i * W:(i + 1) * W], t["bt_bf"][:, (b0 + i) * 128:(b0 + i + 1) * 128],
                                    t["zbuf"][:, 0:W], start=True, stop=True)
                    e.nop(cycle_cnt=_gap(W))
                return last
            p.op("tensor", mm_b, "sPE")

            def addf(e, b0=b0, W=W):
                dst = xdst_fn(b0)
                return e.tensor_add(dst, dst, pb[:, 0:4 * W])
            c_add = p.psum_read("vector", addf, "sDVE")
        return c_add

    def kv_block(u, c_r, c_y, kv_stop=None):
        W = 256
        ybuf = ya[:, 8192:8192 + NDC * W]
        p.wait("sync", "sDVE", c_y)
        cs1 = None
        for slot, name in ((CSs, "d_cos"), (SNs, "d_sin")):
            cs1 = p.op("sync", lambda e, slot=slot, name=name, u=u: e.dma_start(
                out=pool[:, slot:slot + W], in_=t[name][:, u * W:(u + 1) * W]), "sX", 16)
        p.wait("vector", "sX", cs1)
        p.wait("vector", "sACT", c_r)
        p.op("vector", lambda e: e.tensor_mul(pool[:, C2s:C2s + W], pool[:, CSs:CSs + W], pool[:, RBs:RBs + W]), "sDVE")
        c_cs = p.op("vector", lambda e: e.tensor_mul(pool[:, S2s:S2s + W], pool[:, SNs:SNs + W], pool[:, RBs:RBs + W]), "sDVE")

        if kv_stop == -1:
            return
        p.pe_barrier()
        p.wait("tensor", "sDVE", c_y)
        for w_i, wname in enumerate(("d_wk", "d_wv")):
            for half in range(2):
                bi = (2 * w_i + half) % 2
                wait_pair("gpsimd", st["ws_free"][bi])
                c_w = p.op("gpsimd", lambda e, bi=bi, wname=wname, half=half: e.dma_start(
                    out=t[f"ws{bi}"][:], in_=t[wname][:, half * 16:(half + 1) * 16, :]), "sW", 16)
                st["ws_ready"][bi] = ("sW", c_w)
                wait_pair("tensor", st["ws_ready"][bi])

                def mm_kv(e, bi=bi, half=half, w_i=w_i, W=W):
                    last = None
                    for i in range(16):
                        dc = half * 16 + i
                        last = e.matmul(pb[:, w_i * 512:w_i * 512 + W], t[f"ws{bi}"][:, i * 128:(i + 1) * 128],
                                        ybuf[:, dc * W:(dc + 1) * W],
                                        start=(dc == 0), stop=(dc == NDC - 1))
                        e.nop(cycle_cnt=_gap(W))
                    return last
                c_pe = p.op("tensor", mm_kv, "sPE")
                st["ws_free"][bi] = ("sPE", c_pe)

        if kv_stop == 0:
            return
        c_kraw = p.psum_read("scalar", lambda e, W=W: e.copy(t["qk_raw"][:, 0:W], bank(0, W)), "sACT")
        if kv_stop == 11:
            return
        c_vtf = p.psum_read("vector", lambda e, W=W: e.tensor_mul(pool[:, T1s:T1s + W], bank(1, W), pool[:, RBs:RBs + W]), "sDVE")
        if kv_stop == 1:
            return

        p.pe_barrier()
        p.wait("tensor", "sACT", c_kraw)
        p.wait("tensor", "sDVE", c_vtf)

        def mm_rot_tr(e, W=W):
            e.matmul(bank(2, W), t["rot_bf"][:], t["qk_raw"][:, 0:W], start=True, stop=True)
            e.nop(cycle_cnt=_gap(W))
            last = None
            for b in range(2):
                last = e.transpose(pb[:, 3 * 512 + b * 128:3 * 512 + (b + 1) * 128],
                                   pool[:, T1s + b * 128:T1s + (b + 1) * 128], t["ident_f"][:])
                e.nop(cycle_cnt=160)
            return last
        p.op("tensor", mm_rot_tr, "sPE")
        if kv_stop == 2:
            return

        p.wait("vector", "sDVE", c_cs)
        p.wait("vector", "sACT", c_kraw)
        p.op("vector", lambda e, W=W: e.tensor_mul(pool[:, SNs:SNs + W], t["qk_raw"][:, 0:W], pool[:, C2s:C2s + W]), "sDVE")
        p.psum_read("vector", lambda e, W=W: e.tensor_mul(pool[:, VBs:VBs + W], bank(2, W), pool[:, S2s:S2s + W]), "sDVE")
        p.op("vector", lambda e, u=u, W=W: e.tensor_add(t["krT"][:, u * W:(u + 1) * W],
                                                        pool[:, SNs:SNs + W], pool[:, VBs:VBs + W]), "sDVE")
        if kv_stop == 3:
            return
        p.psum_read("scalar", lambda e, u=u: e.copy(t["vrow"][:, 2 * u * 128:2 * u * 128 + 256],
                                                    pb[:, 3 * 512:3 * 512 + 256]), "sACT")

    xup = xh[:, 0:NDC * 256]

    def xup_fn(dc, W=256):
        if dc is None:
            return xup
        return xh[:, dc * W:(dc + 1) * W]

    for u in range(6):
        p.wait("sync", "sDVE", p.cnt["sDVE"])
        p.wait("sync", "sACT", p.cnt["sACT"])
        c_x = p.op("sync", lambda e, u=u: e.dma_start(
            out=xup.rearrange("p (a b) -> p a b", b=256), in_=t["d_xt"][u]), "sX", 16)
        c_r, subs1 = ln_block(xup_fn, 256, ya[:, 8192:16384], ya, qb, xdep=("sX", c_x))
        if stop is not None and stop[0] == "upln" and u == stop[1] - 1:
            _dbg_dump(p, t)
            return
        c_add = ffn_block(256, ya[:, 8192:16384], lambda oc: xh[:, oc * 256:(oc + 4) * 256], c_r, subs1)
        if stop is not None and stop[0] == "upffn" and u == stop[1] - 1:
            _dbg_dump(p, t)
            return
        c_r2, subs2 = ln_block(xup_fn, 256, ya[:, 8192:16384], ya, qb, xdep=("sDVE", c_add))
        if stop is not None and stop[0] == "upln2" and u == stop[1] - 1:
            _dbg_dump(p, t)
            return
        kv_block(u, c_r2, subs2[NDC - 1],
                 kv_stop=(stop[2] if stop is not None and stop[0] == "kv" and u == stop[1] - 1 else None))
        if stop is not None and stop[0] == "kv" and u == stop[1] - 1:
            _dbg_dump(p, t)
            return
        if stop is not None and stop[0] == "up" and u == stop[1] - 1:
            _dbg_dump(p, t)
            return

    for h in range(2):
        def half_dst(k):
            return xh[:].rearrange("p (a b) -> p a b", b=HF)[:, :, k * 256:(k + 1) * 256]

        p.wait("sync", "sDVE", p.cnt["sDVE"])
        p.wait("sync", "sACT", p.cnt["sACT"])
        c_x2 = None
        for k in range(2):
            c_x2 = p.op("sync", lambda e, h=h, k=k: e.dma_start(
                out=half_dst(k), in_=t["d_xt"][1 + 2 * h + k]), "sX", 16)

        def xhf_fn(dc, W=HF):
            if dc is None:
                return xh[:, 0:NDC * W]
            return xh[:, dc * W:(dc + 1) * W]

        c_r, subs1 = ln_block(xhf_fn, HF, ya[:, 0:NDC * HF], ya, qb, xdep=("sX", c_x2))
        c_add = ffn_block(HF, ya[:, 0:NDC * HF], lambda oc: xh[:, oc * HF:(oc + 4) * HF], c_r, subs1)
        c_r2, subs2 = ln_block(xhf_fn, HF, ya[:, 0:NDC * HF], ya, qb, xdep=("sDVE", c_add))
        if stop is not None and stop[0] == "half" and stop[1] == "ffn1":
            _dbg_dump(p, t)
            return

        q0 = HALO + h * HF
        p.wait("sync", "sDVE", subs2[NDC - 1])
        cs1 = None
        for slot, name in ((CSs, "d_cos"), (SNs, "d_sin")):
            cs1 = p.op("sync", lambda e, slot=slot, name=name, q0=q0: e.dma_start(
                out=pool[:, slot:slot + HF], in_=t[name][:, q0:q0 + HF]), "sX", 16)
        p.wait("vector", "sX", cs1)
        p.wait("vector", "sACT", c_r2)
        p.op("vector", lambda e: e.tensor_mul(pool[:, C2s:C2s + HF], pool[:, CSs:CSs + HF], pool[:, RBs:RBs + HF]), "sDVE")
        c_cs = p.op("vector", lambda e: e.tensor_mul(pool[:, S2s:S2s + HF], pool[:, SNs:SNs + HF], pool[:, RBs:RBs + HF]), "sDVE")

        for hd in range(NH):
            par = hd % 2
            for half in range(2):
                bi = half
                wait_pair("gpsimd", st["ws_free"][bi])
                c_w = p.op("gpsimd", lambda e, bi=bi, hd=hd, half=half: e.dma_start(
                    out=t[f"ws{bi}"][:], in_=t["d_wq"][hd, :, half * 16:(half + 1) * 16, :]), "sW", 16)
                st["ws_ready"][bi] = ("sW", c_w)
            p.pe_barrier()
            if hd == 0:
                p.wait("tensor", "sDVE", subs2[NDC - 1])
            for half in range(2):
                bi = half
                wait_pair("tensor", st["ws_ready"][bi])

                def mm_q(e, bi=bi, half=half, par=par):
                    last = None
                    for i in range(16):
                        dc = half * 16 + i
                        last = e.matmul(bank(par), t[f"ws{bi}"][:, i * 128:(i + 1) * 128],
                                        ya[:, dc * HF:(dc + 1) * HF],
                                        start=(dc == 0), stop=(dc == NDC - 1))
                        e.nop(cycle_cnt=_gap(HF))
                    return last
                c_pe = p.op("tensor", mm_q, "sPE")
                st["ws_free"][bi] = ("sPE", c_pe)
            qr_off = par * HF
            c_qraw = p.psum_read("scalar", lambda e, qr_off=qr_off, par=par: e.copy(
                t["qk_raw"][:, qr_off:qr_off + HF], bank(par)), "sACT")
            p.pe_barrier()
            p.wait("tensor", "sACT", c_qraw)

            def mm_rot(e, qr_off=qr_off, par=par):
                last = e.matmul(bank(2 + par), t["rot_bf"][:], t["qk_raw"][:, qr_off:qr_off + HF], start=True, stop=True)
                e.nop(cycle_cnt=_gap(HF))
                return last
            p.op("tensor", mm_rot, "sPE")
            if hd == 0:
                p.wait("vector", "sDVE", c_cs)
            p.wait("vector", "sACT", c_qraw)
            tsl = T1s if par == 0 else VBs
            msl = CSs if par == 0 else SNs
            p.op("vector", lambda e, qr_off=qr_off, tsl=tsl: e.tensor_mul(
                pool[:, tsl:tsl + HF], t["qk_raw"][:, qr_off:qr_off + HF], pool[:, C2s:C2s + HF]), "sDVE")
            p.psum_read("vector", lambda e, msl=msl, par=par: e.tensor_mul(
                pool[:, msl:msl + HF], bank(2 + par), pool[:, S2s:S2s + HF]), "sDVE")
            p.op("vector", lambda e, hd=hd, tsl=tsl, msl=msl: e.tensor_add(
                qb[:, hd * HF:(hd + 1) * HF], pool[:, tsl:tsl + HF], pool[:, msl:msl + HF]), "sDVE")
        if stop is not None and stop[0] == "half" and stop[1] == "q":
            _dbg_dump(p, t)
            return

        c_q_done = p.cnt["sDVE"]
        p.wait("tensor", "sDVE", c_q_done)
        for j2 in range(2):
            j = 2 * h + j2
            aAB = t["aABf"] if j == 0 else t["aABn"]
            sig_c = [None, None]
            for k in range(NH + 2):
                p.pe_barrier()
                if k < NH:
                    par = k % 2
                    qsl = qb[:, k * HF + j2 * 256: k * HF + j2 * 256 + 256]

                    def mm_s(e, j=j, par=par, qsl=qsl):
                        last = None
                        for ti in range(6):
                            tile = 2 * j + ti
                            last = e.matmul(pb[:, par * 1536 + ti * 256: par * 1536 + (ti + 1) * 256],
                                            t["krT"][:, tile * 128:(tile + 1) * 128],
                                            qsl, start=True, stop=True)
                            e.nop(cycle_cnt=_gap(256))
                        return last
                    p.op("tensor", mm_s, "sPE")
                if 0 < k <= NH:
                    km = k - 1
                    parm = km % 2
                    sgm = t["sig0"] if parm == 0 else t["sig1"]
                    p.wait("tensor", "sACT", sig_c[parm])

                    def mm_sw(e, j=j, sgm=sgm):
                        for w_i in range(2):
                            for i in range(4):
                                ti = w_i * 2 + i
                                tile = 2 * j + ti
                                e.matmul(pb[:, 6 * 512 + w_i * 256:6 * 512 + (w_i + 1) * 256],
                                         t["maskone_bf"][:, tile * 128:(tile + 1) * 128],
                                         sgm[:, ti * 256:(ti + 1) * 256],
                                         start=(i == 0), stop=(i == 3))
                                e.nop(cycle_cnt=_gap(256))
                        last = None
                        for w_i in range(2):
                            for i in range(4):
                                ti = w_i * 2 + i
                                tile = 2 * j + ti
                                last = e.matmul(pb[:, 7 * 512 + w_i * 256:7 * 512 + (w_i + 1) * 256],
                                                t["vrow"][:, tile * 128:(tile + 1) * 128],
                                                sgm[:, ti * 256:(ti + 1) * 256],
                                                start=(i == 0), stop=(i == 3))
                                e.nop(cycle_cnt=_gap(256))
                        return last
                    p.op("tensor", mm_sw, "sPE")
                if k < NH:
                    par = k % 2
                    sg = t["sig0"] if par == 0 else t["sig1"]
                    sig_c[par] = p.psum_read("scalar", lambda e, par=par, sg=sg: e.activation(
                        sg[:, 0:1536], pb[:, par * 1536:par * 1536 + 1536], AF.Sigmoid, scale=SCALE), "sACT")
                if 0 < k <= NH:
                    km = k - 1
                    pm = km % 2
                    ssl = (S2s, C2s)[pm]
                    vsl = (VBs, SNs)[pm]
                    c_se = p.psum_read("scalar", lambda e, ssl=ssl: e.copy(pool[:, ssl:ssl + 512], bank(6)), "sACT")
                    p.psum_read("vector", lambda e, aAB=aAB, vsl=vsl: e.tensor_mul(pool[:, vsl:vsl + 512], bank(7), aAB[:]), "sDVE")
                    st["att_se%d" % pm] = c_se
                if k >= 2:
                    km1 = k - 2
                    pm1 = km1 % 2
                    ssl1 = (S2s, C2s)[pm1]
                    vsl1 = (VBs, SNs)[pm1]
                    tsl1 = (T1s, CSs)[pm1]
                    p.wait("vector", "sACT", st["att_se%d" % pm1])
                    p.op("vector", lambda e, ssl1=ssl1, tsl1=tsl1: e.reciprocal(pool[:, tsl1:tsl1 + 512], pool[:, ssl1:ssl1 + 512]), "sDVE")
                    p.op("vector", lambda e, vsl1=vsl1, tsl1=tsl1: e.tensor_mul(pool[:, vsl1:vsl1 + 512], pool[:, vsl1:vsl1 + 512], pool[:, tsl1:tsl1 + 512]), "sDVE")
                    p.op("vector", lambda e, km1=km1, j2=j2, vsl1=vsl1: e.tensor_add(
                        ya[:, km1 * HF + j2 * 256: km1 * HF + j2 * 256 + 256],
                        pool[:, vsl1:vsl1 + 256], pool[:, vsl1 + 256:vsl1 + 512]), "sDVE")
        if stop is not None and stop[0] == "half" and stop[1] == "attn":
            _dbg_dump(p, t)
            return

        c_attn_done = p.cnt["sDVE"]
        p.wait("tensor", "sDVE", c_attn_done)
        for oc in range(NDC):
            par = oc % 2
            for half in range(2):
                bi = half
                wait_pair("gpsimd", st["ws_free"][bi])
                c_w = p.op("gpsimd", lambda e, bi=bi, oc=oc, half=half: e.dma_start(
                    out=t[f"ws{bi}"][:], in_=t["d_wo"][oc, :, half * 16:(half + 1) * 16, :]), "sW", 16)
                st["ws_ready"][bi] = ("sW", c_w)
            if par == 0:
                p.pe_barrier()
            for half in range(2):
                bi = half
                wait_pair("tensor", st["ws_ready"][bi])

                def mm_o(e, bi=bi, half=half, par=par):
                    last = None
                    for i in range(16):
                        dc = half * 16 + i
                        last = e.matmul(bank(par), t[f"ws{bi}"][:, i * 128:(i + 1) * 128],
                                        ya[:, dc * HF:(dc + 1) * HF],
                                        start=(dc == 0), stop=(dc == NDC - 1))
                        e.nop(cycle_cnt=_gap(HF))
                    return last
                c_pe = p.op("tensor", mm_o, "sPE")
                st["ws_free"][bi] = ("sPE", c_pe)
            if par == 1:
                def addo(e, oc=oc):
                    dst = xh[:, (oc - 1) * HF:(oc + 1) * HF]
                    return e.tensor_add(dst, dst, pb[:, 0:1024])
                p.psum_read("vector", addo, "sDVE")
        if stop is not None and stop[0] == "half" and stop[1] == "wo":
            _dbg_dump(p, t)
            return

        c_r3, subs3 = ln_block(xhf_fn, HF, ya[:, 0:NDC * HF], ya, qb,
                               xdep=("sDVE", p.cnt["sDVE"]))
        c_out = ffn_block(HF, ya[:, 0:NDC * HF], lambda oc: xh[:, oc * HF:(oc + 4) * HF], c_r3, subs3)

        p.wait("sync", "sDVE", c_out)
        c_st = p.op("sync", lambda e, h=h: e.dma_start(
            out=t["d_out"][h], in_=xh[:].rearrange("p (a b) -> p a b", b=HF)), "sO", 16)
        if h == 0:
            p.wait("sync", "sO", c_st)

    p.wait("sync", "sO", p.cnt["sO"])


def _rope_tables():
    Dh = DH
    inv_freq = 1.0 / 10000.0 ** (np.arange(0, Dh, 2, dtype=np.float32) / Dh)
    tt = np.arange(S, dtype=np.float32)
    f = np.outer(tt, inv_freq)
    cos = np.zeros((S, Dh), dtype=np.float32)
    sin = np.zeros((S, Dh), dtype=np.float32)
    cos[:, 0::2] = np.cos(f)
    cos[:, 1::2] = np.cos(f)
    sin[:, 0::2] = np.sin(f)
    sin[:, 1::2] = np.sin(f)
    return cos, sin


def _declare(nc):
    t = {}

    def dram(name, shape, dt=F32):
        t["d_" + name] = nc.dram_tensor(name, list(shape), dt, kind="ExternalInput").ap()

    dram("xt", (6, 128, NDC, 256))
    dram("wq", (NH, 128, NDC, 128))
    dram("wo", (NDC, 128, NDC, 128))
    dram("wk", (128, NDC, 128))
    dram("wv", (128, NDC, 128))
    dram("at", (128, NDC * 128))
    dram("wr", (128, NDC * 8))
    dram("bt", (128, D))
    dram("br", (8, 1))
    dram("cos", (128, KRANGE))
    dram("sin", (128, KRANGE))
    dram("ident", (128, 128))
    dram("ones", (128, 128))
    dram("rot", (128, 128))
    dram("e8", (8, 128))
    dram("identf", (128, 128))
    dram("aABf", (128, 512))
    dram("aABn", (128, 512))
    dram("maskone", (128, NKT * 128))
    t["d_out"] = nc.dram_tensor("out", [2, 128, NDC, HF], F32, kind="ExternalOutput").ap()
    t["d_dbg"] = nc.dram_tensor("dbg", [2, 128, KRANGE], BF16, kind="ExternalOutput").ap()
    t["d_dbg2"] = nc.dram_tensor("dbg2", [128, NDC * HF], BF16, kind="ExternalOutput").ap()
    t["d_dbg3"] = nc.dram_tensor("dbg3", [128, NDC * HF], BF16, kind="ExternalOutput").ap()
    return t


def _alloc(nc, t, stack):
    def sb(name, shape, dt):
        t[name] = stack.enter_context(nc.sbuf_tensor("sb_" + name, list(shape), dt))

    sb("ident_bf", (128, 128), BF16)
    sb("ones_bf", (128, 128), BF16)
    sb("rot_bf", (128, 128), BF16)
    sb("e8_bf", (8, 128), BF16)
    sb("ident_f", (128, 128), F32)
    sb("aABf", (128, 512), F32)
    sb("aABn", (128, 512), F32)
    sb("maskone_bf", (128, NKT * 128), BF16)
    sb("brt", (8, 1), F32)
    sb("at_bf", (128, NDC * 128), BF16)
    sb("wr_bf", (128, NDC * 8), BF16)
    sb("bt_bf", (128, D), BF16)
    sb("krT", (128, KRANGE), BF16)
    sb("vrow", (128, KRANGE), BF16)
    sb("xh", (128, NDC * HF), F32)
    sb("ya", (128, NDC * HF), BF16)
    sb("qb", (128, NDC * HF), BF16)
    sb("ws0", (128, 2048), BF16)
    sb("ws1", (128, 2048), BF16)
    sb("qk_raw", (128, 1024), BF16)
    sb("zbuf", (128, HF), BF16)
    sb("sig0", (128, 6 * 256), BF16)
    sb("sig1", (128, 6 * 256), BF16)
    sb("expb", (8, HF), BF16)
    sb("pool", (128, 7 * 512), F32)
    t["pbig"] = stack.enter_context(nc.psum_tensor("pbig", [128, 4096], F32))


def _host_inputs(L0_attn_x, L0_attn_Wq, L0_attn_Wk, L0_attn_Wv, L0_attn_Wo,
                 L0_ffn_A, L0_ffn_B, L0_ffn_Wr, L0_ffn_br):
    x = np.asarray(L0_attn_x, dtype=np.float32)[0]
    Wq = np.ascontiguousarray(np.asarray(L0_attn_Wq, np.float32))
    Wk = np.asarray(L0_attn_Wk, np.float32)
    Wv = np.asarray(L0_attn_Wv, np.float32)
    Wo = np.ascontiguousarray(np.asarray(L0_attn_Wo, np.float32))
    A = np.asarray(L0_ffn_A, np.float32)
    B = np.asarray(L0_ffn_B, np.float32)
    Wr = np.asarray(L0_ffn_Wr, np.float32)
    br = np.asarray(L0_ffn_br, np.float32)

    xT = np.ascontiguousarray(x.T)
    cos, sin = _rope_tables()
    cosT = np.ascontiguousarray(cos.T)
    sinT = np.ascontiguousarray(sin.T)

    wq_l = np.ascontiguousarray(Wq.reshape(NH, 128, NDC, 128).transpose(0, 3, 2, 1))
    wo_l = np.ascontiguousarray(Wo.reshape(NDC, 128, NDC, 128).transpose(0, 3, 2, 1))
    wk_l = np.ascontiguousarray(Wk.reshape(128, NDC, 128).transpose(2, 1, 0))
    wv_l = np.ascontiguousarray(Wv.reshape(128, NDC, 128).transpose(2, 1, 0))
    at_l = np.ascontiguousarray(A.reshape(128, NDC, 128).transpose(2, 1, 0)).reshape(128, NDC * 128)
    wr_l = np.ascontiguousarray(Wr.reshape(8, NDC, 128).transpose(2, 1, 0)).reshape(128, NDC * 8)
    bt_l = np.ascontiguousarray(B.transpose(1, 0, 2).reshape(D, 128).T)
    br_l = br.reshape(8, 1)

    ident = np.eye(128, dtype=np.float32)
    ones = np.ones((128, 128), dtype=np.float32)
    rot = np.zeros((128, 128), dtype=np.float32)
    for i in range(64):
        rot[i + 64, i] = -1.0
        rot[i, i + 64] = 1.0
    e8 = np.zeros((8, 128), dtype=np.float32)
    for hh in range(8):
        e8[hh, hh * 16:(hh + 1) * 16] = 8.0
    alpha = np.linspace(0.0, 1.0, 256, dtype=np.float32)
    aABn = np.zeros((128, 512), dtype=np.float32)
    aABn[:, 0:256] = 1.0 - alpha[None, :]
    aABn[:, 256:512] = alpha[None, :]

    shared = {
        "wq": wq_l, "wo": wo_l, "wk": wk_l, "wv": wv_l,
        "at": at_l, "wr": wr_l, "bt": bt_l, "br": br_l,
        "ident": ident, "identf": ident, "ones": ones, "rot": rot, "e8": e8,
        "aABn": aABn,
    }

    in_maps = []
    for c in range(NCORE):
        g0 = c * OWN - HALO
        xt_c = np.zeros((D, KRANGE), dtype=np.float32)
        lo, hi = max(0, g0), min(S, g0 + KRANGE)
        xt_c[:, lo - g0:hi - g0] = xT[:, lo:hi]
        xt_l = np.ascontiguousarray(
            xt_c.reshape(NDC, 128, 6, 256).transpose(2, 1, 0, 3))
        cos_c = np.zeros((DH, KRANGE), dtype=np.float32)
        sin_c = np.zeros((DH, KRANGE), dtype=np.float32)
        cos_c[:, lo - g0:hi - g0] = cosT[:, lo:hi]
        sin_c[:, lo - g0:hi - g0] = sinT[:, lo:hi]
        mo = np.zeros((128, NKT * 128), dtype=np.float32)
        for tt in range(NKT):
            grow = g0 + tt * 128 + np.arange(128)
            valid = ((grow >= 0) & (grow < S)).astype(np.float32)
            mo[:, tt * 128:(tt + 1) * 128] = valid[:, None]
        if c == 0:
            aABf = np.zeros((128, 512), dtype=np.float32)
            aABf[:, 256:512] = 1.0
        else:
            aABf = aABn
        m = dict(shared)
        m.update({"xt": xt_l, "cos": np.ascontiguousarray(cos_c),
                  "sin": np.ascontiguousarray(sin_c), "maskone": mo,
                  "aABf": np.ascontiguousarray(aABf)})
        in_maps.append(m)
    return in_maps


def kernel(L0_attn_x, L0_attn_Wq, L0_attn_Wk, L0_attn_Wv, L0_attn_Wo,
           L0_ffn_A, L0_ffn_B, L0_ffn_Wr, L0_ffn_br):
    from contextlib import ExitStack

    in_maps = _host_inputs(L0_attn_x, L0_attn_Wq, L0_attn_Wk, L0_attn_Wv,
                           L0_attn_Wo, L0_ffn_A, L0_ffn_B, L0_ffn_Wr, L0_ffn_br)

    nc = bass.Bass("TRN2", target_bir_lowering=False)
    t = _declare(nc)
    p = P()
    with ExitStack() as stack:
        _alloc(nc, t, stack)
        sems = {}
        for s in ("sW", "sX", "sO", "sPE", "sACT", "sDVE", "sPOOL"):
            sems[s] = stack.enter_context(nc.semaphore(s))
        p.sems = sems
        build_program(p, t)
        blk = stack.enter_context(nc.Block())
        for eng_name in P.ENGS:
            closures = p.q[eng_name]

            def make_body(closures):
                def body(eng):
                    for fn in closures:
                        fn(eng)
                return body
            getattr(blk, eng_name)(make_body(closures))

    res = run_bass_kernel_spmd(nc, in_maps, core_ids=list(range(NCORE)))
    outs = []
    for c in range(NCORE):
        o = res.results[c]["out"]
        outs.append(o.transpose(2, 1, 0, 3).reshape(D, OWN))
    full = np.concatenate(outs, axis=1)
    return np.ascontiguousarray(full.T).reshape(1, S, D).astype(np.float32)


# version 09
# speedup vs baseline: 1.1665x; 1.0150x over previous
import numpy as np
import concourse.bass as bass
import concourse.mybir as mybir
from concourse.bass_utils import run_bass_kernel_spmd

F32 = mybir.dt.float32
BF16 = mybir.dt.bfloat16
AF = mybir.ActivationFunctionType
ALU = mybir.AluOpType

NCORE = 8
S = 8192
D = 4096
NH = 32
DH = 128
OWN = 1024
HALO = 256
KRANGE = OWN + 2 * HALO
NKT = KRANGE // 128
NDC = D // 128
HF = 512
NEG = -30000.0
SCALE = 1.0 / float(np.sqrt(DH))
EPS = 1e-5

CS, SN, C2, S2, T1, VB, MB, RB = [i * 512 for i in range(8)]


def _gap(n):
    if n >= 384:
        return 144
    if n >= 192:
        return 76
    return 56


class P:
    ENGS = ("tensor", "scalar", "vector", "gpsimd", "sync")

    def __init__(self):
        self.q = {e: [] for e in self.ENGS}
        self.cnt = {s: 0 for s in ("sW", "sX", "sO", "sPE", "sACT", "sDVE", "sPOOL")}
        self.sems = {}
        self.pending = {}
        self.keepwarm = None

    def wait(self, eng, sem, val):
        if val is None or val <= 0:
            return
        assert val <= self.cnt[sem], f"wait {sem}>={val}, recorded {self.cnt[sem]}"
        self.q[eng].append(lambda e, s=sem, v=val: e.wait_ge(self.sems[s], v))

    def op(self, eng, fn, inc=None, by=1):
        v = None
        if inc is not None:
            self.cnt[inc] += by
            v = self.cnt[inc]

        def g(e, fn=fn, inc=inc, by=by):
            ins = fn(e)
            if inc is not None:
                ins.then_inc(self.sems[inc], by)
        self.q[eng].append(g)
        return v

    def raw(self, eng, fn):
        self.q[eng].append(fn)

    def psum_read(self, eng, fn, inc):
        self.wait(eng, "sPE", self.cnt["sPE"])
        v = self.op(eng, fn, inc)
        self.pending[inc] = v
        return v

    def pe_barrier(self):
        if self.pending and self.keepwarm is not None:
            kw = self.keepwarm

            def dummies(e, kw=kw):
                for _ in range(16):
                    e.ldweights(kw[:])
            self.raw("tensor", dummies)
        for sem, v in self.pending.items():
            self.wait("tensor", sem, v)
        self.pending = {}





def _dbg_dump(p, t, extra=()):
    for eng in ("sync",):
        p.wait(eng, "sDVE", p.cnt["sDVE"])
        p.wait(eng, "sACT", p.cnt["sACT"])
        p.wait(eng, "sPE", p.cnt["sPE"])
    p.op("sync", lambda e: e.dma_start(out=t["d_out"][0], in_=t["xh"][:].rearrange("p (a b) -> p a b", b=HF)), "sO", 16)
    p.op("sync", lambda e: e.dma_start(out=t["d_dbg"][0], in_=t["krT"][:]), "sO", 16)
    p.op("sync", lambda e: e.dma_start(out=t["d_dbg"][1], in_=t["vrow"][:]), "sO", 16)
    p.op("sync", lambda e: e.dma_start(out=t["d_dbg2"], in_=t["ya"][:]), "sO", 16)
    p.op("sync", lambda e: e.dma_start(out=t["d_dbg3"], in_=t["qb"][:]), "sO", 16)
    p.wait("sync", "sO", p.cnt["sO"])


def build_program(p, t, stop=None):

    for name_d, name_s in [("ident_bf", "d_ident"), ("ones_bf", "d_ones"),
                           ("rot_bf", "d_rot"), ("e8_bf", "d_e8"),
                           ("at_bf", "d_at"), ("wr_bf", "d_wr"), ("bt_bf", "d_bt"),
                           ("maskone_bf", "d_maskone")]:
        p.op("gpsimd", lambda e, d=t[name_d], s=t[name_s]: e.dma_start(out=d[:], in_=s), "sW", 16)
    for name_d, name_s in [("aABf", "d_aABf"), ("aABn", "d_aABn"),
                           ("brt", "d_br"), ("ident_f", "d_identf")]:
        p.op("sync", lambda e, d=t[name_d], s=t[name_s]: e.dma_start(out=d[:], in_=s), "sX", 16)
    cW0, cX0 = p.cnt["sW"], p.cnt["sX"]
    for eng in ("tensor", "scalar", "vector"):
        p.wait(eng, "sW", cW0)
        p.wait(eng, "sX", cX0)

    pool = t["pool"]
    ya, qb, xh, pb = t["ya"], t["qb"], t["xh"], t["pbig"]
    p.keepwarm = t["ident_bf"]

    def bank(i, w=512):
        return pb[:, i * 512:i * 512 + w]

    CSs, SNs, C2s, S2s, T1s, VBs, RBs = 0, 512, 1024, 1536, 2048, 2560, 3072

    st = {
        "ws_free": [("sPE", 0), ("sPE", 0)],
        "ws_ready": [("sW", 0), ("sW", 0)],
    }

    def wait_pair(eng, pair):
        p.wait(eng, pair[0], pair[1])

    def ln_block(xview_fn, W, ybuf, sa, sb, xdep=None):
        n = NDC * W
        full = xview_fn(None)
        if xdep:
            p.wait("vector", xdep[0], xdep[1])
        c_xbf = p.op("vector", lambda e: e.tensor_copy(sa[:, 0:n], full), "sDVE")
        if xdep:
            p.wait("scalar", xdep[0], xdep[1])
        c_xsq = p.op("scalar", lambda e: e.activation(sb[:, 0:n], full, AF.Square), "sACT")

        p.pe_barrier()
        p.wait("tensor", "sDVE", c_xbf)
        p.wait("tensor", "sACT", c_xsq)

        def mms(e):
            last = None
            for bi, src in ((0, sa), (1, sb)):
                for dc in range(NDC):
                    last = e.matmul(pb[:, bi * 512:bi * 512 + W], t["ones_bf"][:], src[:, dc * W:(dc + 1) * W],
                                    start=(dc == 0), stop=(dc == NDC - 1))
                    e.nop(cycle_cnt=_gap(W))
            return last
        p.op("tensor", mms, "sPE")

        p.psum_read("vector", lambda e: e.tensor_scalar_mul(pool[:, CSs:CSs + W], bank(0, W), 1.0 / D), "sDVE")
        p.psum_read("vector", lambda e: e.tensor_scalar(pool[:, VBs:VBs + W], bank(1, W), 1.0 / D, EPS, ALU.mult, ALU.add), "sDVE")
        p.op("vector", lambda e: e.tensor_mul(pool[:, T1s:T1s + W], pool[:, CSs:CSs + W], pool[:, CSs:CSs + W]), "sDVE")
        c_var = p.op("vector", lambda e: e.tensor_sub(pool[:, VBs:VBs + W], pool[:, VBs:VBs + W], pool[:, T1s:T1s + W]), "sDVE")
        p.wait("scalar", "sDVE", c_var)
        p.op("scalar", lambda e: e.activation(pool[:, T1s:T1s + W], pool[:, VBs:VBs + W], AF.Ln), "sACT")
        c_r = p.op("scalar", lambda e: e.activation(pool[:, RBs:RBs + W], pool[:, T1s:T1s + W], AF.Exp, scale=-0.5), "sACT")
        subs = []
        for dc in range(NDC):
            def sub_fn(e, dc=dc):
                return e.tensor_sub(ybuf[:, dc * W:(dc + 1) * W], xview_fn(dc), pool[:, CSs:CSs + W])
            subs.append(p.op("vector", sub_fn, "sDVE"))
        return c_r, subs

    def ffn_block(W, ybuf, xdst_fn, c_r, subs):
        p.pe_barrier()

        def mm_lg_low(e):
            for dc in range(NDC):
                if dc % 4 == 0:
                    e.wait_ge(p.sems["sDVE"], subs[min(dc + 3, NDC - 1)])
                e.matmul(pb[0:8, 2 * 512:2 * 512 + W], t["wr_bf"][:, dc * 8:(dc + 1) * 8],
                         ybuf[:, dc * W:(dc + 1) * W], start=(dc == 0), stop=(dc == NDC - 1))
                e.nop(cycle_cnt=_gap(W))
            last = None
            for dc in range(NDC):
                last = e.matmul(pb[:, 3 * 512:3 * 512 + W], t["at_bf"][:, dc * 128:(dc + 1) * 128],
                                ybuf[:, dc * W:(dc + 1) * W], start=(dc == 0), stop=(dc == NDC - 1))
                e.nop(cycle_cnt=_gap(W))
            return last
        p.op("tensor", mm_lg_low, "sPE")

        p.wait("vector", "sACT", c_r)
        c_l2 = p.psum_read("vector", lambda e: e.tensor_mul(pool[:, CSs:CSs + W], bank(2, W), pool[:, RBs:RBs + W]), "sDVE")
        p.wait("scalar", "sDVE", c_l2)
        c_exp = p.op("scalar", lambda e: e.activation(t["expb"][0:8, 0:W], pool[0:8, CSs:CSs + W], AF.Exp,
                                                      bias=t["brt"][0:8, :]), "sACT")

        p.pe_barrier()
        p.wait("tensor", "sACT", c_exp)

        def mm_e(e):
            e.matmul(bank(4, W), t["ones_bf"][0:8, :], t["expb"][0:8, 0:W], start=True, stop=True)
            e.nop(cycle_cnt=_gap(W))
            last = e.matmul(bank(5, W), t["e8_bf"][:], t["expb"][0:8, 0:W], start=True, stop=True)
            e.nop(cycle_cnt=_gap(W))
            return last
        p.op("tensor", mm_e, "sPE")

        p.psum_read("vector", lambda e: e.tensor_mul(pool[:, VBs:VBs + W], bank(5, W), pool[:, RBs:RBs + W]), "sDVE")
        p.psum_read("vector", lambda e: e.reciprocal(pool[:, T1s:T1s + W], bank(4, W)), "sDVE")
        p.op("vector", lambda e: e.tensor_mul(pool[:, VBs:VBs + W], pool[:, VBs:VBs + W], pool[:, T1s:T1s + W]), "sDVE")
        p.psum_read("vector", lambda e: e.tensor_mul(t["zbuf"][:, 0:W], bank(3, W), pool[:, VBs:VBs + W]), "sDVE")

        c_add = None
        for b0 in range(0, NDC, 4):
            p.pe_barrier()

            def mm_b(e, b0=b0, W=W):
                last = None
                for i in range(4):
                    last = e.matmul(pb[:, i * W:(i + 1) * W], t["bt_bf"][:, (b0 + i) * 128:(b0 + i + 1) * 128],
                                    t["zbuf"][:, 0:W], start=True, stop=True)
                    e.nop(cycle_cnt=_gap(W))
                return last
            p.op("tensor", mm_b, "sPE")

            def addf(e, b0=b0, W=W):
                dst = xdst_fn(b0)
                return e.tensor_add(dst, dst, pb[:, 0:4 * W])
            c_add = p.psum_read("vector", addf, "sDVE")
        return c_add

    def kv_block(u, c_r, c_y, kv_stop=None):
        W = 256
        ybuf = ya[:, 8192:8192 + NDC * W]
        p.wait("sync", "sDVE", c_y)
        cs1 = None
        for slot, name in ((CSs, "d_cos"), (SNs, "d_sin")):
            cs1 = p.op("sync", lambda e, slot=slot, name=name, u=u: e.dma_start(
                out=pool[:, slot:slot + W], in_=t[name][:, u * W:(u + 1) * W]), "sX", 16)
        p.wait("vector", "sX", cs1)
        p.wait("vector", "sACT", c_r)
        p.op("vector", lambda e: e.tensor_mul(pool[:, C2s:C2s + W], pool[:, CSs:CSs + W], pool[:, RBs:RBs + W]), "sDVE")
        c_cs = p.op("vector", lambda e: e.tensor_mul(pool[:, S2s:S2s + W], pool[:, SNs:SNs + W], pool[:, RBs:RBs + W]), "sDVE")

        if kv_stop == -1:
            return
        p.pe_barrier()
        p.wait("tensor", "sDVE", c_y)
        for w_i, wname in enumerate(("d_wk", "d_wv")):
            for half in range(2):
                bi = (2 * w_i + half) % 2
                wait_pair("gpsimd", st["ws_free"][bi])
                c_w = p.op("gpsimd", lambda e, bi=bi, wname=wname, half=half: e.dma_start(
                    out=t[f"ws{bi}"][:], in_=t[wname][:, half * 16:(half + 1) * 16, :]), "sW", 16)
                st["ws_ready"][bi] = ("sW", c_w)
                wait_pair("tensor", st["ws_ready"][bi])

                def mm_kv(e, bi=bi, half=half, w_i=w_i, W=W):
                    last = None
                    for i in range(16):
                        dc = half * 16 + i
                        last = e.matmul(pb[:, w_i * 512:w_i * 512 + W], t[f"ws{bi}"][:, i * 128:(i + 1) * 128],
                                        ybuf[:, dc * W:(dc + 1) * W],
                                        start=(dc == 0), stop=(dc == NDC - 1))
                        e.nop(cycle_cnt=_gap(W))
                    return last
                c_pe = p.op("tensor", mm_kv, "sPE")
                st["ws_free"][bi] = ("sPE", c_pe)

        if kv_stop == 0:
            return
        c_kraw = p.psum_read("scalar", lambda e, W=W: e.copy(t["qk_raw"][:, 0:W], bank(0, W)), "sACT")
        if kv_stop == 11:
            return
        c_vtf = p.psum_read("vector", lambda e, W=W: e.tensor_mul(pool[:, T1s:T1s + W], bank(1, W), pool[:, RBs:RBs + W]), "sDVE")
        if kv_stop == 1:
            return

        p.pe_barrier()
        p.wait("tensor", "sACT", c_kraw)
        p.wait("tensor", "sDVE", c_vtf)

        def mm_rot_tr(e, W=W):
            e.matmul(bank(2, W), t["rot_bf"][:], t["qk_raw"][:, 0:W], start=True, stop=True)
            e.nop(cycle_cnt=_gap(W))
            last = None
            for b in range(2):
                last = e.transpose(pb[:, 3 * 512 + b * 128:3 * 512 + (b + 1) * 128],
                                   pool[:, T1s + b * 128:T1s + (b + 1) * 128], t["ident_f"][:])
                e.nop(cycle_cnt=160)
            return last
        p.op("tensor", mm_rot_tr, "sPE")
        if kv_stop == 2:
            return

        p.wait("vector", "sDVE", c_cs)
        p.wait("vector", "sACT", c_kraw)
        p.op("vector", lambda e, W=W: e.tensor_mul(pool[:, SNs:SNs + W], t["qk_raw"][:, 0:W], pool[:, C2s:C2s + W]), "sDVE")
        p.psum_read("vector", lambda e, W=W: e.tensor_mul(pool[:, VBs:VBs + W], bank(2, W), pool[:, S2s:S2s + W]), "sDVE")
        p.op("vector", lambda e, u=u, W=W: e.tensor_add(t["krT"][:, u * W:(u + 1) * W],
                                                        pool[:, SNs:SNs + W], pool[:, VBs:VBs + W]), "sDVE")
        if kv_stop == 3:
            return
        p.psum_read("scalar", lambda e, u=u: e.copy(t["vrow"][:, 2 * u * 128:2 * u * 128 + 256],
                                                    pb[:, 3 * 512:3 * 512 + 256]), "sACT")

    xup = xh[:, 0:NDC * 256]

    def xup_fn(dc, W=256):
        if dc is None:
            return xup
        return xh[:, dc * W:(dc + 1) * W]

    xq = [None] * 7

    def emit_xload(u):
        p.wait("sync", "sDVE", p.cnt["sDVE"])
        p.wait("sync", "sACT", p.cnt["sACT"])
        return p.op("sync", lambda e, u=u: e.dma_start(
            out=xup.rearrange("p (a b) -> p a b", b=256), in_=t["d_xt"][u]), "sX", 16)

    xq[0] = emit_xload(0)
    for u in range(6):
        c_x = xq[u]
        c_r, subs1 = ln_block(xup_fn, 256, ya[:, 8192:16384], ya, qb, xdep=("sX", c_x))
        if stop is not None and stop[0] == "upln" and u == stop[1] - 1:
            _dbg_dump(p, t)
            return
        c_add = ffn_block(256, ya[:, 8192:16384], lambda oc: xh[:, oc * 256:(oc + 4) * 256], c_r, subs1)
        if stop is not None and stop[0] == "upffn" and u == stop[1] - 1:
            _dbg_dump(p, t)
            return
        c_r2, subs2 = ln_block(xup_fn, 256, ya[:, 8192:16384], ya, qb, xdep=("sDVE", c_add))
        if stop is not None and stop[0] == "upln2" and u == stop[1] - 1:
            _dbg_dump(p, t)
            return
        if u < 5:
            p.wait("sync", "sDVE", subs2[NDC - 1])
            p.wait("sync", "sACT", c_r2)
            xq[u + 1] = p.op("sync", lambda e, u=u: e.dma_start(
                out=xup.rearrange("p (a b) -> p a b", b=256), in_=t["d_xt"][u + 1]), "sX", 16)
        kv_block(u, c_r2, subs2[NDC - 1],
                 kv_stop=(stop[2] if stop is not None and stop[0] == "kv" and u == stop[1] - 1 else None))
        if stop is not None and stop[0] == "kv" and u == stop[1] - 1:
            _dbg_dump(p, t)
            return
        if stop is not None and stop[0] == "up" and u == stop[1] - 1:
            _dbg_dump(p, t)
            return

    for h in range(2):
        def half_dst(k):
            return xh[:].rearrange("p (a b) -> p a b", b=HF)[:, :, k * 256:(k + 1) * 256]

        p.wait("sync", "sDVE", p.cnt["sDVE"])
        p.wait("sync", "sACT", p.cnt["sACT"])
        c_x2 = None
        for k in range(2):
            c_x2 = p.op("sync", lambda e, h=h, k=k: e.dma_start(
                out=half_dst(k), in_=t["d_xt"][1 + 2 * h + k]), "sX", 16)

        def xhf_fn(dc, W=HF):
            if dc is None:
                return xh[:, 0:NDC * W]
            return xh[:, dc * W:(dc + 1) * W]

        c_r, subs1 = ln_block(xhf_fn, HF, ya[:, 0:NDC * HF], ya, qb, xdep=("sX", c_x2))
        c_add = ffn_block(HF, ya[:, 0:NDC * HF], lambda oc: xh[:, oc * HF:(oc + 4) * HF], c_r, subs1)
        c_r2, subs2 = ln_block(xhf_fn, HF, ya[:, 0:NDC * HF], ya, qb, xdep=("sDVE", c_add))
        if stop is not None and stop[0] == "half" and stop[1] == "ffn1":
            _dbg_dump(p, t)
            return

        q0 = HALO + h * HF
        p.wait("sync", "sDVE", subs2[NDC - 1])
        cs1 = None
        for slot, name in ((CSs, "d_cos"), (SNs, "d_sin")):
            cs1 = p.op("sync", lambda e, slot=slot, name=name, q0=q0: e.dma_start(
                out=pool[:, slot:slot + HF], in_=t[name][:, q0:q0 + HF]), "sX", 16)
        p.wait("vector", "sX", cs1)
        p.wait("vector", "sACT", c_r2)
        p.op("vector", lambda e: e.tensor_mul(pool[:, C2s:C2s + HF], pool[:, CSs:CSs + HF], pool[:, RBs:RBs + HF]), "sDVE")
        c_cs = p.op("vector", lambda e: e.tensor_mul(pool[:, S2s:S2s + HF], pool[:, SNs:SNs + HF], pool[:, RBs:RBs + HF]), "sDVE")

        for hd in range(NH):
            par = hd % 2
            for half in range(2):
                bi = half
                wait_pair("gpsimd", st["ws_free"][bi])
                c_w = p.op("gpsimd", lambda e, bi=bi, hd=hd, half=half: e.dma_start(
                    out=t[f"ws{bi}"][:], in_=t["d_wq"][hd, :, half * 16:(half + 1) * 16, :]), "sW", 16)
                st["ws_ready"][bi] = ("sW", c_w)
            p.pe_barrier()
            if hd == 0:
                p.wait("tensor", "sDVE", subs2[NDC - 1])
            for half in range(2):
                bi = half
                wait_pair("tensor", st["ws_ready"][bi])

                def mm_q(e, bi=bi, half=half, par=par):
                    last = None
                    for i in range(16):
                        dc = half * 16 + i
                        last = e.matmul(bank(par), t[f"ws{bi}"][:, i * 128:(i + 1) * 128],
                                        ya[:, dc * HF:(dc + 1) * HF],
                                        start=(dc == 0), stop=(dc == NDC - 1))
                        e.nop(cycle_cnt=_gap(HF))
                    return last
                c_pe = p.op("tensor", mm_q, "sPE")
                st["ws_free"][bi] = ("sPE", c_pe)
            qr_off = par * HF
            c_qraw = p.psum_read("scalar", lambda e, qr_off=qr_off, par=par: e.copy(
                t["qk_raw"][:, qr_off:qr_off + HF], bank(par)), "sACT")
            p.pe_barrier()
            p.wait("tensor", "sACT", c_qraw)

            def mm_rot(e, qr_off=qr_off, par=par):
                last = e.matmul(bank(2 + par), t["rot_bf"][:], t["qk_raw"][:, qr_off:qr_off + HF], start=True, stop=True)
                e.nop(cycle_cnt=_gap(HF))
                return last
            p.op("tensor", mm_rot, "sPE")
            if hd == 0:
                p.wait("vector", "sDVE", c_cs)
            p.wait("vector", "sACT", c_qraw)
            tsl = T1s if par == 0 else VBs
            msl = CSs if par == 0 else SNs
            p.op("vector", lambda e, qr_off=qr_off, tsl=tsl: e.tensor_mul(
                pool[:, tsl:tsl + HF], t["qk_raw"][:, qr_off:qr_off + HF], pool[:, C2s:C2s + HF]), "sDVE")
            p.psum_read("vector", lambda e, msl=msl, par=par: e.tensor_mul(
                pool[:, msl:msl + HF], bank(2 + par), pool[:, S2s:S2s + HF]), "sDVE")
            p.op("vector", lambda e, hd=hd, tsl=tsl, msl=msl: e.tensor_add(
                qb[:, hd * HF:(hd + 1) * HF], pool[:, tsl:tsl + HF], pool[:, msl:msl + HF]), "sDVE")
        if stop is not None and stop[0] == "half" and stop[1] == "q":
            _dbg_dump(p, t)
            return

        c_q_done = p.cnt["sDVE"]
        p.wait("tensor", "sDVE", c_q_done)
        for j2 in range(2):
            j = 2 * h + j2
            aAB = t["aABf"] if j == 0 else t["aABn"]
            sig_c = [None, None]
            for k in range(NH + 2):
                p.pe_barrier()
                if k < NH:
                    par = k % 2
                    qsl = qb[:, k * HF + j2 * 256: k * HF + j2 * 256 + 256]

                    def mm_s(e, j=j, par=par, qsl=qsl):
                        last = None
                        for ti in range(6):
                            tile = 2 * j + ti
                            last = e.matmul(pb[:, par * 1536 + ti * 256: par * 1536 + (ti + 1) * 256],
                                            t["krT"][:, tile * 128:(tile + 1) * 128],
                                            qsl, start=True, stop=True)
                            e.nop(cycle_cnt=_gap(256))
                        return last
                    p.op("tensor", mm_s, "sPE")
                if 0 < k <= NH:
                    km = k - 1
                    parm = km % 2
                    sgm = t["sig0"] if parm == 0 else t["sig1"]
                    p.wait("tensor", "sACT", sig_c[parm])

                    def mm_sw(e, j=j, sgm=sgm):
                        for w_i in range(2):
                            for i in range(4):
                                ti = w_i * 2 + i
                                tile = 2 * j + ti
                                e.matmul(pb[:, 6 * 512 + w_i * 256:6 * 512 + (w_i + 1) * 256],
                                         t["maskone_bf"][:, tile * 128:(tile + 1) * 128],
                                         sgm[:, ti * 256:(ti + 1) * 256],
                                         start=(i == 0), stop=(i == 3))
                                e.nop(cycle_cnt=_gap(256))
                        last = None
                        for w_i in range(2):
                            for i in range(4):
                                ti = w_i * 2 + i
                                tile = 2 * j + ti
                                last = e.matmul(pb[:, 7 * 512 + w_i * 256:7 * 512 + (w_i + 1) * 256],
                                                t["vrow"][:, tile * 128:(tile + 1) * 128],
                                                sgm[:, ti * 256:(ti + 1) * 256],
                                                start=(i == 0), stop=(i == 3))
                                e.nop(cycle_cnt=_gap(256))
                        return last
                    p.op("tensor", mm_sw, "sPE")
                if k < NH:
                    par = k % 2
                    sg = t["sig0"] if par == 0 else t["sig1"]
                    sig_c[par] = p.psum_read("scalar", lambda e, par=par, sg=sg: e.activation(
                        sg[:, 0:1536], pb[:, par * 1536:par * 1536 + 1536], AF.Sigmoid, scale=SCALE), "sACT")
                if 0 < k <= NH:
                    km = k - 1
                    pm = km % 2
                    ssl = (S2s, C2s)[pm]
                    vsl = (VBs, SNs)[pm]
                    c_se = p.psum_read("scalar", lambda e, ssl=ssl: e.copy(pool[:, ssl:ssl + 512], bank(6)), "sACT")
                    p.psum_read("vector", lambda e, aAB=aAB, vsl=vsl: e.tensor_mul(pool[:, vsl:vsl + 512], bank(7), aAB[:]), "sDVE")
                    st["att_se%d" % pm] = c_se
                if k >= 2:
                    km1 = k - 2
                    pm1 = km1 % 2
                    ssl1 = (S2s, C2s)[pm1]
                    vsl1 = (VBs, SNs)[pm1]
                    tsl1 = (T1s, CSs)[pm1]
                    p.wait("vector", "sACT", st["att_se%d" % pm1])
                    p.op("vector", lambda e, ssl1=ssl1, tsl1=tsl1: e.reciprocal(pool[:, tsl1:tsl1 + 512], pool[:, ssl1:ssl1 + 512]), "sDVE")
                    p.op("vector", lambda e, vsl1=vsl1, tsl1=tsl1: e.tensor_mul(pool[:, vsl1:vsl1 + 512], pool[:, vsl1:vsl1 + 512], pool[:, tsl1:tsl1 + 512]), "sDVE")
                    p.op("vector", lambda e, km1=km1, j2=j2, vsl1=vsl1: e.tensor_add(
                        ya[:, km1 * HF + j2 * 256: km1 * HF + j2 * 256 + 256],
                        pool[:, vsl1:vsl1 + 256], pool[:, vsl1 + 256:vsl1 + 512]), "sDVE")
        if stop is not None and stop[0] == "half" and stop[1] == "attn":
            _dbg_dump(p, t)
            return

        c_attn_done = p.cnt["sDVE"]
        p.wait("tensor", "sDVE", c_attn_done)
        for oc in range(NDC):
            par = oc % 2
            for half in range(2):
                bi = half
                wait_pair("gpsimd", st["ws_free"][bi])
                c_w = p.op("gpsimd", lambda e, bi=bi, oc=oc, half=half: e.dma_start(
                    out=t[f"ws{bi}"][:], in_=t["d_wo"][oc, :, half * 16:(half + 1) * 16, :]), "sW", 16)
                st["ws_ready"][bi] = ("sW", c_w)
            if par == 0:
                p.pe_barrier()
            for half in range(2):
                bi = half
                wait_pair("tensor", st["ws_ready"][bi])

                def mm_o(e, bi=bi, half=half, par=par):
                    last = None
                    for i in range(16):
                        dc = half * 16 + i
                        last = e.matmul(bank(par), t[f"ws{bi}"][:, i * 128:(i + 1) * 128],
                                        ya[:, dc * HF:(dc + 1) * HF],
                                        start=(dc == 0), stop=(dc == NDC - 1))
                        e.nop(cycle_cnt=_gap(HF))
                    return last
                c_pe = p.op("tensor", mm_o, "sPE")
                st["ws_free"][bi] = ("sPE", c_pe)
            if par == 1:
                def addo(e, oc=oc):
                    dst = xh[:, (oc - 1) * HF:(oc + 1) * HF]
                    return e.tensor_add(dst, dst, pb[:, 0:1024])
                p.psum_read("vector", addo, "sDVE")
        if stop is not None and stop[0] == "half" and stop[1] == "wo":
            _dbg_dump(p, t)
            return

        c_r3, subs3 = ln_block(xhf_fn, HF, ya[:, 0:NDC * HF], ya, qb,
                               xdep=("sDVE", p.cnt["sDVE"]))
        c_out = ffn_block(HF, ya[:, 0:NDC * HF], lambda oc: xh[:, oc * HF:(oc + 4) * HF], c_r3, subs3)

        p.wait("sync", "sDVE", c_out)
        c_st = p.op("sync", lambda e, h=h: e.dma_start(
            out=t["d_out"][h], in_=xh[:].rearrange("p (a b) -> p a b", b=HF)), "sO", 16)
        if h == 0:
            p.wait("sync", "sO", c_st)

    p.wait("sync", "sO", p.cnt["sO"])


def _rope_tables():
    Dh = DH
    inv_freq = 1.0 / 10000.0 ** (np.arange(0, Dh, 2, dtype=np.float32) / Dh)
    tt = np.arange(S, dtype=np.float32)
    f = np.outer(tt, inv_freq)
    cos = np.zeros((S, Dh), dtype=np.float32)
    sin = np.zeros((S, Dh), dtype=np.float32)
    cos[:, 0::2] = np.cos(f)
    cos[:, 1::2] = np.cos(f)
    sin[:, 0::2] = np.sin(f)
    sin[:, 1::2] = np.sin(f)
    return cos, sin


def _declare(nc):
    t = {}

    def dram(name, shape, dt=F32):
        t["d_" + name] = nc.dram_tensor(name, list(shape), dt, kind="ExternalInput").ap()

    dram("xt", (6, 128, NDC, 256))
    dram("wq", (NH, 128, NDC, 128))
    dram("wo", (NDC, 128, NDC, 128))
    dram("wk", (128, NDC, 128))
    dram("wv", (128, NDC, 128))
    dram("at", (128, NDC * 128))
    dram("wr", (128, NDC * 8))
    dram("bt", (128, D))
    dram("br", (8, 1))
    dram("cos", (128, KRANGE))
    dram("sin", (128, KRANGE))
    dram("ident", (128, 128))
    dram("ones", (128, 128))
    dram("rot", (128, 128))
    dram("e8", (8, 128))
    dram("identf", (128, 128))
    dram("aABf", (128, 512))
    dram("aABn", (128, 512))
    dram("maskone", (128, NKT * 128))
    t["d_out"] = nc.dram_tensor("out", [2, 128, NDC, HF], F32, kind="ExternalOutput").ap()
    t["d_dbg"] = nc.dram_tensor("dbg", [2, 128, KRANGE], BF16, kind="ExternalOutput").ap()
    t["d_dbg2"] = nc.dram_tensor("dbg2", [128, NDC * HF], BF16, kind="ExternalOutput").ap()
    t["d_dbg3"] = nc.dram_tensor("dbg3", [128, NDC * HF], BF16, kind="ExternalOutput").ap()
    return t


def _alloc(nc, t, stack):
    def sb(name, shape, dt):
        t[name] = stack.enter_context(nc.sbuf_tensor("sb_" + name, list(shape), dt))

    sb("ident_bf", (128, 128), BF16)
    sb("ones_bf", (128, 128), BF16)
    sb("rot_bf", (128, 128), BF16)
    sb("e8_bf", (8, 128), BF16)
    sb("ident_f", (128, 128), F32)
    sb("aABf", (128, 512), F32)
    sb("aABn", (128, 512), F32)
    sb("maskone_bf", (128, NKT * 128), BF16)
    sb("brt", (8, 1), F32)
    sb("at_bf", (128, NDC * 128), BF16)
    sb("wr_bf", (128, NDC * 8), BF16)
    sb("bt_bf", (128, D), BF16)
    sb("krT", (128, KRANGE), BF16)
    sb("vrow", (128, KRANGE), BF16)
    sb("xh", (128, NDC * HF), F32)
    sb("ya", (128, NDC * HF), BF16)
    sb("qb", (128, NDC * HF), BF16)
    sb("ws0", (128, 2048), BF16)
    sb("ws1", (128, 2048), BF16)
    sb("qk_raw", (128, 1024), BF16)
    sb("zbuf", (128, HF), BF16)
    sb("sig0", (128, 6 * 256), BF16)
    sb("sig1", (128, 6 * 256), BF16)
    sb("expb", (8, HF), BF16)
    sb("pool", (128, 7 * 512), F32)
    t["pbig"] = stack.enter_context(nc.psum_tensor("pbig", [128, 4096], F32))


def _host_inputs(L0_attn_x, L0_attn_Wq, L0_attn_Wk, L0_attn_Wv, L0_attn_Wo,
                 L0_ffn_A, L0_ffn_B, L0_ffn_Wr, L0_ffn_br):
    x = np.asarray(L0_attn_x, dtype=np.float32)[0]
    Wq = np.ascontiguousarray(np.asarray(L0_attn_Wq, np.float32))
    Wk = np.asarray(L0_attn_Wk, np.float32)
    Wv = np.asarray(L0_attn_Wv, np.float32)
    Wo = np.ascontiguousarray(np.asarray(L0_attn_Wo, np.float32))
    A = np.asarray(L0_ffn_A, np.float32)
    B = np.asarray(L0_ffn_B, np.float32)
    Wr = np.asarray(L0_ffn_Wr, np.float32)
    br = np.asarray(L0_ffn_br, np.float32)

    xT = np.ascontiguousarray(x.T)
    cos, sin = _rope_tables()
    cosT = np.ascontiguousarray(cos.T)
    sinT = np.ascontiguousarray(sin.T)

    wq_l = np.ascontiguousarray(Wq.reshape(NH, 128, NDC, 128).transpose(0, 3, 2, 1))
    wo_l = np.ascontiguousarray(Wo.reshape(NDC, 128, NDC, 128).transpose(0, 3, 2, 1))
    wk_l = np.ascontiguousarray(Wk.reshape(128, NDC, 128).transpose(2, 1, 0))
    wv_l = np.ascontiguousarray(Wv.reshape(128, NDC, 128).transpose(2, 1, 0))
    at_l = np.ascontiguousarray(A.reshape(128, NDC, 128).transpose(2, 1, 0)).reshape(128, NDC * 128)
    wr_l = np.ascontiguousarray(Wr.reshape(8, NDC, 128).transpose(2, 1, 0)).reshape(128, NDC * 8)
    bt_l = np.ascontiguousarray(B.transpose(1, 0, 2).reshape(D, 128).T)
    br_l = br.reshape(8, 1)

    ident = np.eye(128, dtype=np.float32)
    ones = np.ones((128, 128), dtype=np.float32)
    rot = np.zeros((128, 128), dtype=np.float32)
    for i in range(64):
        rot[i + 64, i] = -1.0
        rot[i, i + 64] = 1.0
    e8 = np.zeros((8, 128), dtype=np.float32)
    for hh in range(8):
        e8[hh, hh * 16:(hh + 1) * 16] = 8.0
    alpha = np.linspace(0.0, 1.0, 256, dtype=np.float32)
    aABn = np.zeros((128, 512), dtype=np.float32)
    aABn[:, 0:256] = 1.0 - alpha[None, :]
    aABn[:, 256:512] = alpha[None, :]

    shared = {
        "wq": wq_l, "wo": wo_l, "wk": wk_l, "wv": wv_l,
        "at": at_l, "wr": wr_l, "bt": bt_l, "br": br_l,
        "ident": ident, "identf": ident, "ones": ones, "rot": rot, "e8": e8,
        "aABn": aABn,
    }

    in_maps = []
    for c in range(NCORE):
        g0 = c * OWN - HALO
        xt_c = np.zeros((D, KRANGE), dtype=np.float32)
        lo, hi = max(0, g0), min(S, g0 + KRANGE)
        xt_c[:, lo - g0:hi - g0] = xT[:, lo:hi]
        xt_l = np.ascontiguousarray(
            xt_c.reshape(NDC, 128, 6, 256).transpose(2, 1, 0, 3))
        cos_c = np.zeros((DH, KRANGE), dtype=np.float32)
        sin_c = np.zeros((DH, KRANGE), dtype=np.float32)
        cos_c[:, lo - g0:hi - g0] = cosT[:, lo:hi]
        sin_c[:, lo - g0:hi - g0] = sinT[:, lo:hi]
        mo = np.zeros((128, NKT * 128), dtype=np.float32)
        for tt in range(NKT):
            grow = g0 + tt * 128 + np.arange(128)
            valid = ((grow >= 0) & (grow < S)).astype(np.float32)
            mo[:, tt * 128:(tt + 1) * 128] = valid[:, None]
        if c == 0:
            aABf = np.zeros((128, 512), dtype=np.float32)
            aABf[:, 256:512] = 1.0
        else:
            aABf = aABn
        m = dict(shared)
        m.update({"xt": xt_l, "cos": np.ascontiguousarray(cos_c),
                  "sin": np.ascontiguousarray(sin_c), "maskone": mo,
                  "aABf": np.ascontiguousarray(aABf)})
        in_maps.append(m)
    return in_maps


def kernel(L0_attn_x, L0_attn_Wq, L0_attn_Wk, L0_attn_Wv, L0_attn_Wo,
           L0_ffn_A, L0_ffn_B, L0_ffn_Wr, L0_ffn_br):
    from contextlib import ExitStack

    in_maps = _host_inputs(L0_attn_x, L0_attn_Wq, L0_attn_Wk, L0_attn_Wv,
                           L0_attn_Wo, L0_ffn_A, L0_ffn_B, L0_ffn_Wr, L0_ffn_br)

    nc = bass.Bass("TRN2", target_bir_lowering=False)
    t = _declare(nc)
    p = P()
    with ExitStack() as stack:
        _alloc(nc, t, stack)
        sems = {}
        for s in ("sW", "sX", "sO", "sPE", "sACT", "sDVE", "sPOOL"):
            sems[s] = stack.enter_context(nc.semaphore(s))
        p.sems = sems
        build_program(p, t)
        blk = stack.enter_context(nc.Block())
        for eng_name in P.ENGS:
            closures = p.q[eng_name]

            def make_body(closures):
                def body(eng):
                    for fn in closures:
                        fn(eng)
                return body
            getattr(blk, eng_name)(make_body(closures))

    res = run_bass_kernel_spmd(nc, in_maps, core_ids=list(range(NCORE)))
    outs = []
    for c in range(NCORE):
        o = res.results[c]["out"]
        outs.append(o.transpose(2, 1, 0, 3).reshape(D, OWN))
    full = np.concatenate(outs, axis=1)
    return np.ascontiguousarray(full.T).reshape(1, S, D).astype(np.float32)


# version 10
# speedup vs baseline: 1.2054x; 1.0333x over previous
import numpy as np
import concourse.bass as bass
import concourse.mybir as mybir
from concourse.bass_utils import run_bass_kernel_spmd

F32 = mybir.dt.float32
BF16 = mybir.dt.bfloat16
AF = mybir.ActivationFunctionType
ALU = mybir.AluOpType

NCORE = 8
S = 8192
D = 4096
NH = 32
DH = 128
OWN = 1024
HALO = 256
KRANGE = OWN + 2 * HALO
NKT = KRANGE // 128
NDC = D // 128
HF = 512
NEG = -30000.0
SCALE = 1.0 / float(np.sqrt(DH))
EPS = 1e-5

CS, SN, C2, S2, T1, VB, MB, RB = [i * 512 for i in range(8)]


def _gap(n):
    if n >= 384:
        return 144
    if n >= 192:
        return 76
    return 56


class P:
    ENGS = ("tensor", "scalar", "vector", "gpsimd", "sync")

    def __init__(self):
        self.q = {e: [] for e in self.ENGS}
        self.cnt = {s: 0 for s in ("sW", "sX", "sO", "sPE", "sACT", "sDVE", "sPOOL")}
        self.sems = {}
        self.pending = {}
        self.keepwarm = None

    def wait(self, eng, sem, val):
        if val is None or val <= 0:
            return
        assert val <= self.cnt[sem], f"wait {sem}>={val}, recorded {self.cnt[sem]}"
        self.q[eng].append(lambda e, s=sem, v=val: e.wait_ge(self.sems[s], v))

    def op(self, eng, fn, inc=None, by=1):
        v = None
        if inc is not None:
            self.cnt[inc] += by
            v = self.cnt[inc]

        def g(e, fn=fn, inc=inc, by=by):
            ins = fn(e)
            if inc is not None:
                ins.then_inc(self.sems[inc], by)
        self.q[eng].append(g)
        return v

    def raw(self, eng, fn):
        self.q[eng].append(fn)

    def psum_read(self, eng, fn, inc):
        self.wait(eng, "sPE", self.cnt["sPE"])
        v = self.op(eng, fn, inc)
        self.pending[inc] = v
        return v

    def pe_barrier(self):
        if self.pending and self.keepwarm is not None:
            kw = self.keepwarm

            def dummies(e, kw=kw):
                for _ in range(16):
                    e.ldweights(kw[:])
            self.raw("tensor", dummies)
        for sem, v in self.pending.items():
            self.wait("tensor", sem, v)
        self.pending = {}





def _dbg_dump(p, t, extra=()):
    for eng in ("sync",):
        p.wait(eng, "sDVE", p.cnt["sDVE"])
        p.wait(eng, "sACT", p.cnt["sACT"])
        p.wait(eng, "sPE", p.cnt["sPE"])
    p.op("sync", lambda e: e.dma_start(out=t["d_out"][0], in_=t["xh"][:].rearrange("p (a b) -> p a b", b=HF)), "sO", 16)
    p.op("sync", lambda e: e.dma_start(out=t["d_dbg"][0], in_=t["krT"][:]), "sO", 16)
    p.op("sync", lambda e: e.dma_start(out=t["d_dbg"][1], in_=t["vrow"][:]), "sO", 16)
    p.op("sync", lambda e: e.dma_start(out=t["d_dbg2"], in_=t["ya"][:]), "sO", 16)
    p.op("sync", lambda e: e.dma_start(out=t["d_dbg3"], in_=t["qb"][:]), "sO", 16)
    p.wait("sync", "sO", p.cnt["sO"])


def build_program(p, t, stop=None):

    for name_d, name_s in [("ident_bf", "d_ident"), ("ones_bf", "d_ones"),
                           ("rot_bf", "d_rot"), ("e8_bf", "d_e8"),
                           ("at_bf", "d_at"), ("wr_bf", "d_wr"), ("bt_bf", "d_bt"),
                           ("maskone_bf", "d_maskone")]:
        p.op("gpsimd", lambda e, d=t[name_d], s=t[name_s]: e.dma_start(out=d[:], in_=s), "sW", 16)
    for name_d, name_s in [("aABf", "d_aABf"), ("aABn", "d_aABn"),
                           ("brt", "d_br"), ("ident_f", "d_identf")]:
        p.op("sync", lambda e, d=t[name_d], s=t[name_s]: e.dma_start(out=d[:], in_=s), "sX", 16)
    cW0, cX0 = p.cnt["sW"], p.cnt["sX"]
    for eng in ("tensor", "scalar", "vector"):
        p.wait(eng, "sW", cW0)
        p.wait(eng, "sX", cX0)

    pool = t["pool"]
    ya, qb, xh, pb = t["ya"], t["qb"], t["xh"], t["pbig"]
    p.keepwarm = t["ident_bf"]

    def bank(i, w=512):
        return pb[:, i * 512:i * 512 + w]

    CSs, SNs, C2s, S2s, T1s, VBs, RBs = 0, 512, 1024, 1536, 2048, 2560, 3072

    st = {
        "ws_free": [("sPE", 0), ("sPE", 0)],
        "ws_ready": [("sW", 0), ("sW", 0)],
    }

    def wait_pair(eng, pair):
        p.wait(eng, pair[0], pair[1])

    def ln_block(xview_fn, W, ybuf, sa, sb, xdep=None):
        n = NDC * W
        full = xview_fn(None)
        if xdep:
            p.wait("vector", xdep[0], xdep[1])
        c_xbf = p.op("vector", lambda e: e.tensor_copy(sa[:, 0:n], full), "sDVE")
        if xdep:
            p.wait("scalar", xdep[0], xdep[1])
        c_xsq = p.op("scalar", lambda e: e.activation(sb[:, 0:n], full, AF.Square), "sACT")

        p.pe_barrier()
        p.wait("tensor", "sDVE", c_xbf)
        p.wait("tensor", "sACT", c_xsq)

        def mms(e):
            last = None
            for bi, src in ((0, sa), (1, sb)):
                for dc in range(NDC):
                    last = e.matmul(pb[:, bi * 512:bi * 512 + W], t["ones_bf"][:], src[:, dc * W:(dc + 1) * W],
                                    start=(dc == 0), stop=(dc == NDC - 1))
                    e.nop(cycle_cnt=_gap(W))
            return last
        p.op("tensor", mms, "sPE")

        p.psum_read("vector", lambda e: e.tensor_scalar_mul(pool[:, CSs:CSs + W], bank(0, W), 1.0 / D), "sDVE")
        p.psum_read("vector", lambda e: e.tensor_scalar(pool[:, VBs:VBs + W], bank(1, W), 1.0 / D, EPS, ALU.mult, ALU.add), "sDVE")
        p.op("vector", lambda e: e.tensor_mul(pool[:, T1s:T1s + W], pool[:, CSs:CSs + W], pool[:, CSs:CSs + W]), "sDVE")
        c_var = p.op("vector", lambda e: e.tensor_sub(pool[:, VBs:VBs + W], pool[:, VBs:VBs + W], pool[:, T1s:T1s + W]), "sDVE")
        p.wait("scalar", "sDVE", c_var)
        p.op("scalar", lambda e: e.activation(pool[:, T1s:T1s + W], pool[:, VBs:VBs + W], AF.Ln), "sACT")
        c_r = p.op("scalar", lambda e: e.activation(pool[:, RBs:RBs + W], pool[:, T1s:T1s + W], AF.Exp, scale=-0.5), "sACT")
        subs = []
        for dc in range(NDC):
            def sub_fn(e, dc=dc):
                return e.tensor_sub(ybuf[:, dc * W:(dc + 1) * W], xview_fn(dc), pool[:, CSs:CSs + W])
            subs.append(p.op("vector", sub_fn, "sDVE"))
        return c_r, subs

    def ffn_block(W, ybuf, xdst_fn, c_r, subs, store_h=None):
        p.pe_barrier()

        def mm_lg_low(e):
            for dc in range(NDC):
                if dc % 4 == 0:
                    e.wait_ge(p.sems["sDVE"], subs[min(dc + 3, NDC - 1)])
                e.matmul(pb[0:8, 2 * 512:2 * 512 + W], t["wr_bf"][:, dc * 8:(dc + 1) * 8],
                         ybuf[:, dc * W:(dc + 1) * W], start=(dc == 0), stop=(dc == NDC - 1))
                e.nop(cycle_cnt=_gap(W))
            last = None
            for dc in range(NDC):
                last = e.matmul(pb[:, 3 * 512:3 * 512 + W], t["at_bf"][:, dc * 128:(dc + 1) * 128],
                                ybuf[:, dc * W:(dc + 1) * W], start=(dc == 0), stop=(dc == NDC - 1))
                e.nop(cycle_cnt=_gap(W))
            return last
        p.op("tensor", mm_lg_low, "sPE")

        p.wait("vector", "sACT", c_r)
        c_l2 = p.psum_read("vector", lambda e: e.tensor_mul(pool[:, CSs:CSs + W], bank(2, W), pool[:, RBs:RBs + W]), "sDVE")
        p.wait("scalar", "sDVE", c_l2)
        c_exp = p.op("scalar", lambda e: e.activation(t["expb"][0:8, 0:W], pool[0:8, CSs:CSs + W], AF.Exp,
                                                      bias=t["brt"][0:8, :]), "sACT")

        p.pe_barrier()
        p.wait("tensor", "sACT", c_exp)

        def mm_e(e):
            e.matmul(bank(4, W), t["ones_bf"][0:8, :], t["expb"][0:8, 0:W], start=True, stop=True)
            e.nop(cycle_cnt=_gap(W))
            last = e.matmul(bank(5, W), t["e8_bf"][:], t["expb"][0:8, 0:W], start=True, stop=True)
            e.nop(cycle_cnt=_gap(W))
            return last
        p.op("tensor", mm_e, "sPE")

        p.psum_read("vector", lambda e: e.tensor_mul(pool[:, VBs:VBs + W], bank(5, W), pool[:, RBs:RBs + W]), "sDVE")
        p.psum_read("vector", lambda e: e.reciprocal(pool[:, T1s:T1s + W], bank(4, W)), "sDVE")
        p.op("vector", lambda e: e.tensor_mul(pool[:, VBs:VBs + W], pool[:, VBs:VBs + W], pool[:, T1s:T1s + W]), "sDVE")
        p.psum_read("vector", lambda e: e.tensor_mul(t["zbuf"][:, 0:W], bank(3, W), pool[:, VBs:VBs + W]), "sDVE")

        c_add = None
        for b0 in range(0, NDC, 4):
            p.pe_barrier()

            def mm_b(e, b0=b0, W=W):
                last = None
                for i in range(4):
                    last = e.matmul(pb[:, i * W:(i + 1) * W], t["bt_bf"][:, (b0 + i) * 128:(b0 + i + 1) * 128],
                                    t["zbuf"][:, 0:W], start=True, stop=True)
                    e.nop(cycle_cnt=_gap(W))
                return last
            p.op("tensor", mm_b, "sPE")

            def addf(e, b0=b0, W=W):
                dst = xdst_fn(b0)
                return e.tensor_add(dst, dst, pb[:, 0:4 * W])
            c_add = p.psum_read("vector", addf, "sDVE")
            if store_h is not None:
                p.wait("sync", "sDVE", c_add)
                p.op("sync", lambda e, b0=b0, store_h=store_h: e.dma_start(
                    out=t["d_out"][store_h][:, b0:b0 + 4, :],
                    in_=xh[:, b0 * HF:(b0 + 4) * HF].rearrange("p (a b) -> p a b", b=HF)), "sO", 16)
        return c_add

    def kv_block(u, c_r, c_y, kv_stop=None):
        W = 256
        ybuf = ya[:, 8192:8192 + NDC * W]
        p.wait("sync", "sDVE", c_y)
        cs1 = None
        for slot, name in ((CSs, "d_cos"), (SNs, "d_sin")):
            cs1 = p.op("sync", lambda e, slot=slot, name=name, u=u: e.dma_start(
                out=pool[:, slot:slot + W], in_=t[name][:, u * W:(u + 1) * W]), "sX", 16)
        p.wait("vector", "sX", cs1)
        p.wait("vector", "sACT", c_r)
        p.op("vector", lambda e: e.tensor_mul(pool[:, C2s:C2s + W], pool[:, CSs:CSs + W], pool[:, RBs:RBs + W]), "sDVE")
        c_cs = p.op("vector", lambda e: e.tensor_mul(pool[:, S2s:S2s + W], pool[:, SNs:SNs + W], pool[:, RBs:RBs + W]), "sDVE")

        if kv_stop == -1:
            return
        p.pe_barrier()
        p.wait("tensor", "sDVE", c_y)
        for w_i, wname in enumerate(("d_wk", "d_wv")):
            for half in range(2):
                bi = (2 * w_i + half) % 2
                wait_pair("gpsimd", st["ws_free"][bi])
                c_w = p.op("gpsimd", lambda e, bi=bi, wname=wname, half=half: e.dma_start(
                    out=t[f"ws{bi}"][:], in_=t[wname][:, half * 16:(half + 1) * 16, :]), "sW", 16)
                st["ws_ready"][bi] = ("sW", c_w)
                wait_pair("tensor", st["ws_ready"][bi])

                def mm_kv(e, bi=bi, half=half, w_i=w_i, W=W):
                    last = None
                    for i in range(16):
                        dc = half * 16 + i
                        last = e.matmul(pb[:, w_i * 512:w_i * 512 + W], t[f"ws{bi}"][:, i * 128:(i + 1) * 128],
                                        ybuf[:, dc * W:(dc + 1) * W],
                                        start=(dc == 0), stop=(dc == NDC - 1))
                        e.nop(cycle_cnt=_gap(W))
                    return last
                c_pe = p.op("tensor", mm_kv, "sPE")
                st["ws_free"][bi] = ("sPE", c_pe)

        if kv_stop == 0:
            return
        c_kraw = p.psum_read("scalar", lambda e, W=W: e.copy(t["qk_raw"][:, 0:W], bank(0, W)), "sACT")
        if kv_stop == 11:
            return
        c_vtf = p.psum_read("vector", lambda e, W=W: e.tensor_mul(pool[:, T1s:T1s + W], bank(1, W), pool[:, RBs:RBs + W]), "sDVE")
        if kv_stop == 1:
            return

        p.pe_barrier()
        p.wait("tensor", "sACT", c_kraw)
        p.wait("tensor", "sDVE", c_vtf)

        def mm_rot_tr(e, W=W):
            e.matmul(bank(2, W), t["rot_bf"][:], t["qk_raw"][:, 0:W], start=True, stop=True)
            e.nop(cycle_cnt=_gap(W))
            last = None
            for b in range(2):
                last = e.transpose(pb[:, 3 * 512 + b * 128:3 * 512 + (b + 1) * 128],
                                   pool[:, T1s + b * 128:T1s + (b + 1) * 128], t["ident_f"][:])
                e.nop(cycle_cnt=160)
            return last
        p.op("tensor", mm_rot_tr, "sPE")
        if kv_stop == 2:
            return

        p.wait("vector", "sDVE", c_cs)
        p.wait("vector", "sACT", c_kraw)
        p.op("vector", lambda e, W=W: e.tensor_mul(pool[:, SNs:SNs + W], t["qk_raw"][:, 0:W], pool[:, C2s:C2s + W]), "sDVE")
        p.psum_read("vector", lambda e, W=W: e.tensor_mul(pool[:, VBs:VBs + W], bank(2, W), pool[:, S2s:S2s + W]), "sDVE")
        p.op("vector", lambda e, u=u, W=W: e.tensor_add(t["krT"][:, u * W:(u + 1) * W],
                                                        pool[:, SNs:SNs + W], pool[:, VBs:VBs + W]), "sDVE")
        if kv_stop == 3:
            return
        p.psum_read("scalar", lambda e, u=u: e.copy(t["vrow"][:, 2 * u * 128:2 * u * 128 + 256],
                                                    pb[:, 3 * 512:3 * 512 + 256]), "sACT")

    xup = xh[:, 0:NDC * 256]

    def xup_fn(dc, W=256):
        if dc is None:
            return xup
        return xh[:, dc * W:(dc + 1) * W]

    xq = [None] * 7

    def emit_xload(u):
        p.wait("sync", "sDVE", p.cnt["sDVE"])
        p.wait("sync", "sACT", p.cnt["sACT"])
        return p.op("sync", lambda e, u=u: e.dma_start(
            out=xup.rearrange("p (a b) -> p a b", b=256), in_=t["d_xt"][u]), "sX", 16)

    xq[0] = emit_xload(0)
    for u in range(6):
        c_x = xq[u]
        c_r, subs1 = ln_block(xup_fn, 256, ya[:, 8192:16384], ya, qb, xdep=("sX", c_x))
        if stop is not None and stop[0] == "upln" and u == stop[1] - 1:
            _dbg_dump(p, t)
            return
        c_add = ffn_block(256, ya[:, 8192:16384], lambda oc: xh[:, oc * 256:(oc + 4) * 256], c_r, subs1)
        if stop is not None and stop[0] == "upffn" and u == stop[1] - 1:
            _dbg_dump(p, t)
            return
        c_r2, subs2 = ln_block(xup_fn, 256, ya[:, 8192:16384], ya, qb, xdep=("sDVE", c_add))
        if stop is not None and stop[0] == "upln2" and u == stop[1] - 1:
            _dbg_dump(p, t)
            return
        if u < 5:
            p.wait("sync", "sDVE", subs2[NDC - 1])
            p.wait("sync", "sACT", c_r2)
            xq[u + 1] = p.op("sync", lambda e, u=u: e.dma_start(
                out=xup.rearrange("p (a b) -> p a b", b=256), in_=t["d_xt"][u + 1]), "sX", 16)
        kv_block(u, c_r2, subs2[NDC - 1],
                 kv_stop=(stop[2] if stop is not None and stop[0] == "kv" and u == stop[1] - 1 else None))
        if stop is not None and stop[0] == "kv" and u == stop[1] - 1:
            _dbg_dump(p, t)
            return
        if stop is not None and stop[0] == "up" and u == stop[1] - 1:
            _dbg_dump(p, t)
            return

    for h in range(2):
        def half_dst(k):
            return xh[:].rearrange("p (a b) -> p a b", b=HF)[:, :, k * 256:(k + 1) * 256]

        p.wait("sync", "sDVE", p.cnt["sDVE"])
        p.wait("sync", "sACT", p.cnt["sACT"])
        c_x2 = None
        for k in range(2):
            c_x2 = p.op("sync", lambda e, h=h, k=k: e.dma_start(
                out=half_dst(k), in_=t["d_xt"][1 + 2 * h + k]), "sX", 16)

        def xhf_fn(dc, W=HF):
            if dc is None:
                return xh[:, 0:NDC * W]
            return xh[:, dc * W:(dc + 1) * W]

        c_r, subs1 = ln_block(xhf_fn, HF, ya[:, 0:NDC * HF], ya, qb, xdep=("sX", c_x2))
        c_add = ffn_block(HF, ya[:, 0:NDC * HF], lambda oc: xh[:, oc * HF:(oc + 4) * HF], c_r, subs1)
        c_r2, subs2 = ln_block(xhf_fn, HF, ya[:, 0:NDC * HF], ya, qb, xdep=("sDVE", c_add))
        if stop is not None and stop[0] == "half" and stop[1] == "ffn1":
            _dbg_dump(p, t)
            return

        q0 = HALO + h * HF
        p.wait("sync", "sDVE", subs2[NDC - 1])
        cs1 = None
        for slot, name in ((CSs, "d_cos"), (SNs, "d_sin")):
            cs1 = p.op("sync", lambda e, slot=slot, name=name, q0=q0: e.dma_start(
                out=pool[:, slot:slot + HF], in_=t[name][:, q0:q0 + HF]), "sX", 16)
        p.wait("vector", "sX", cs1)
        p.wait("vector", "sACT", c_r2)
        p.op("vector", lambda e: e.tensor_mul(pool[:, C2s:C2s + HF], pool[:, CSs:CSs + HF], pool[:, RBs:RBs + HF]), "sDVE")
        c_cs = p.op("vector", lambda e: e.tensor_mul(pool[:, S2s:S2s + HF], pool[:, SNs:SNs + HF], pool[:, RBs:RBs + HF]), "sDVE")

        for hd in range(NH):
            par = hd % 2
            for half in range(2):
                bi = half
                wait_pair("gpsimd", st["ws_free"][bi])
                c_w = p.op("gpsimd", lambda e, bi=bi, hd=hd, half=half: e.dma_start(
                    out=t[f"ws{bi}"][:], in_=t["d_wq"][hd, :, half * 16:(half + 1) * 16, :]), "sW", 16)
                st["ws_ready"][bi] = ("sW", c_w)
            p.pe_barrier()
            if hd == 0:
                p.wait("tensor", "sDVE", subs2[NDC - 1])
            for half in range(2):
                bi = half
                wait_pair("tensor", st["ws_ready"][bi])

                def mm_q(e, bi=bi, half=half, par=par):
                    last = None
                    for i in range(16):
                        dc = half * 16 + i
                        last = e.matmul(bank(par), t[f"ws{bi}"][:, i * 128:(i + 1) * 128],
                                        ya[:, dc * HF:(dc + 1) * HF],
                                        start=(dc == 0), stop=(dc == NDC - 1))
                        e.nop(cycle_cnt=_gap(HF))
                    return last
                c_pe = p.op("tensor", mm_q, "sPE")
                st["ws_free"][bi] = ("sPE", c_pe)
            qr_off = par * HF
            c_qraw = p.psum_read("scalar", lambda e, qr_off=qr_off, par=par: e.copy(
                t["qk_raw"][:, qr_off:qr_off + HF], bank(par)), "sACT")
            p.pe_barrier()
            p.wait("tensor", "sACT", c_qraw)

            def mm_rot(e, qr_off=qr_off, par=par):
                last = e.matmul(bank(2 + par), t["rot_bf"][:], t["qk_raw"][:, qr_off:qr_off + HF], start=True, stop=True)
                e.nop(cycle_cnt=_gap(HF))
                return last
            p.op("tensor", mm_rot, "sPE")
            if hd == 0:
                p.wait("vector", "sDVE", c_cs)
            p.wait("vector", "sACT", c_qraw)
            tsl = T1s if par == 0 else VBs
            msl = CSs if par == 0 else SNs
            p.op("vector", lambda e, qr_off=qr_off, tsl=tsl: e.tensor_mul(
                pool[:, tsl:tsl + HF], t["qk_raw"][:, qr_off:qr_off + HF], pool[:, C2s:C2s + HF]), "sDVE")
            p.psum_read("vector", lambda e, msl=msl, par=par: e.tensor_mul(
                pool[:, msl:msl + HF], bank(2 + par), pool[:, S2s:S2s + HF]), "sDVE")
            p.op("vector", lambda e, hd=hd, tsl=tsl, msl=msl: e.tensor_add(
                qb[:, hd * HF:(hd + 1) * HF], pool[:, tsl:tsl + HF], pool[:, msl:msl + HF]), "sDVE")
        if stop is not None and stop[0] == "half" and stop[1] == "q":
            _dbg_dump(p, t)
            return

        c_q_done = p.cnt["sDVE"]
        p.wait("tensor", "sDVE", c_q_done)
        for j2 in range(2):
            j = 2 * h + j2
            aAB = t["aABf"] if j == 0 else t["aABn"]
            sig_c = [None, None]
            for k in range(NH + 2):
                p.pe_barrier()
                if k < NH:
                    par = k % 2
                    qsl = qb[:, k * HF + j2 * 256: k * HF + j2 * 256 + 256]

                    def mm_s(e, j=j, par=par, qsl=qsl):
                        last = None
                        for ti in range(6):
                            tile = 2 * j + ti
                            last = e.matmul(pb[:, par * 1536 + ti * 256: par * 1536 + (ti + 1) * 256],
                                            t["krT"][:, tile * 128:(tile + 1) * 128],
                                            qsl, start=True, stop=True)
                            e.nop(cycle_cnt=_gap(256))
                        return last
                    p.op("tensor", mm_s, "sPE")
                if 0 < k <= NH:
                    km = k - 1
                    parm = km % 2
                    sgm = t["sig0"] if parm == 0 else t["sig1"]
                    p.wait("tensor", "sACT", sig_c[parm])

                    def mm_sw(e, j=j, sgm=sgm):
                        for w_i in range(2):
                            for i in range(4):
                                ti = w_i * 2 + i
                                tile = 2 * j + ti
                                e.matmul(pb[:, 6 * 512 + w_i * 256:6 * 512 + (w_i + 1) * 256],
                                         t["maskone_bf"][:, tile * 128:(tile + 1) * 128],
                                         sgm[:, ti * 256:(ti + 1) * 256],
                                         start=(i == 0), stop=(i == 3))
                                e.nop(cycle_cnt=_gap(256))
                        last = None
                        for w_i in range(2):
                            for i in range(4):
                                ti = w_i * 2 + i
                                tile = 2 * j + ti
                                last = e.matmul(pb[:, 7 * 512 + w_i * 256:7 * 512 + (w_i + 1) * 256],
                                                t["vrow"][:, tile * 128:(tile + 1) * 128],
                                                sgm[:, ti * 256:(ti + 1) * 256],
                                                start=(i == 0), stop=(i == 3))
                                e.nop(cycle_cnt=_gap(256))
                        return last
                    p.op("tensor", mm_sw, "sPE")
                if k < NH:
                    par = k % 2
                    sg = t["sig0"] if par == 0 else t["sig1"]
                    sig_c[par] = p.psum_read("scalar", lambda e, par=par, sg=sg: e.activation(
                        sg[:, 0:1536], pb[:, par * 1536:par * 1536 + 1536], AF.Sigmoid, scale=SCALE), "sACT")
                if 0 < k <= NH:
                    km = k - 1
                    pm = km % 2
                    ssl = (S2s, C2s)[pm]
                    vsl = (VBs, SNs)[pm]
                    c_se = p.psum_read("scalar", lambda e, ssl=ssl: e.copy(pool[:, ssl:ssl + 512], bank(6)), "sACT")
                    p.psum_read("vector", lambda e, aAB=aAB, vsl=vsl: e.tensor_mul(pool[:, vsl:vsl + 512], bank(7), aAB[:]), "sDVE")
                    st["att_se%d" % pm] = c_se
                if k >= 2:
                    km1 = k - 2
                    pm1 = km1 % 2
                    ssl1 = (S2s, C2s)[pm1]
                    vsl1 = (VBs, SNs)[pm1]
                    tsl1 = (T1s, CSs)[pm1]
                    p.wait("vector", "sACT", st["att_se%d" % pm1])
                    p.op("vector", lambda e, ssl1=ssl1, tsl1=tsl1: e.reciprocal(pool[:, tsl1:tsl1 + 512], pool[:, ssl1:ssl1 + 512]), "sDVE")
                    p.op("vector", lambda e, vsl1=vsl1, tsl1=tsl1: e.tensor_mul(pool[:, vsl1:vsl1 + 512], pool[:, vsl1:vsl1 + 512], pool[:, tsl1:tsl1 + 512]), "sDVE")
                    p.op("vector", lambda e, km1=km1, j2=j2, vsl1=vsl1: e.tensor_add(
                        ya[:, km1 * HF + j2 * 256: km1 * HF + j2 * 256 + 256],
                        pool[:, vsl1:vsl1 + 256], pool[:, vsl1 + 256:vsl1 + 512]), "sDVE")
        if stop is not None and stop[0] == "half" and stop[1] == "attn":
            _dbg_dump(p, t)
            return

        c_attn_done = p.cnt["sDVE"]
        p.wait("tensor", "sDVE", c_attn_done)
        for oc in range(NDC):
            par = oc % 2
            for half in range(2):
                bi = half
                wait_pair("gpsimd", st["ws_free"][bi])
                c_w = p.op("gpsimd", lambda e, bi=bi, oc=oc, half=half: e.dma_start(
                    out=t[f"ws{bi}"][:], in_=t["d_wo"][oc, :, half * 16:(half + 1) * 16, :]), "sW", 16)
                st["ws_ready"][bi] = ("sW", c_w)
            if par == 0:
                p.pe_barrier()
            for half in range(2):
                bi = half
                wait_pair("tensor", st["ws_ready"][bi])

                def mm_o(e, bi=bi, half=half, par=par):
                    last = None
                    for i in range(16):
                        dc = half * 16 + i
                        last = e.matmul(bank(par), t[f"ws{bi}"][:, i * 128:(i + 1) * 128],
                                        ya[:, dc * HF:(dc + 1) * HF],
                                        start=(dc == 0), stop=(dc == NDC - 1))
                        e.nop(cycle_cnt=_gap(HF))
                    return last
                c_pe = p.op("tensor", mm_o, "sPE")
                st["ws_free"][bi] = ("sPE", c_pe)
            if par == 1:
                def addo(e, oc=oc):
                    dst = xh[:, (oc - 1) * HF:(oc + 1) * HF]
                    return e.tensor_add(dst, dst, pb[:, 0:1024])
                p.psum_read("vector", addo, "sDVE")
        if stop is not None and stop[0] == "half" and stop[1] == "wo":
            _dbg_dump(p, t)
            return

        c_r3, subs3 = ln_block(xhf_fn, HF, ya[:, 0:NDC * HF], ya, qb,
                               xdep=("sDVE", p.cnt["sDVE"]))
        c_out = ffn_block(HF, ya[:, 0:NDC * HF], lambda oc: xh[:, oc * HF:(oc + 4) * HF],
                          c_r3, subs3, store_h=h)
        if h == 0:
            p.wait("sync", "sO", p.cnt["sO"])

    p.wait("sync", "sO", p.cnt["sO"])


def _rope_tables():
    Dh = DH
    inv_freq = 1.0 / 10000.0 ** (np.arange(0, Dh, 2, dtype=np.float32) / Dh)
    tt = np.arange(S, dtype=np.float32)
    f = np.outer(tt, inv_freq)
    cos = np.zeros((S, Dh), dtype=np.float32)
    sin = np.zeros((S, Dh), dtype=np.float32)
    cos[:, 0::2] = np.cos(f)
    cos[:, 1::2] = np.cos(f)
    sin[:, 0::2] = np.sin(f)
    sin[:, 1::2] = np.sin(f)
    return cos, sin


def _declare(nc):
    t = {}

    def dram(name, shape, dt=F32):
        t["d_" + name] = nc.dram_tensor(name, list(shape), dt, kind="ExternalInput").ap()

    dram("xt", (6, 128, NDC, 256))
    dram("wq", (NH, 128, NDC, 128))
    dram("wo", (NDC, 128, NDC, 128))
    dram("wk", (128, NDC, 128))
    dram("wv", (128, NDC, 128))
    dram("at", (128, NDC * 128))
    dram("wr", (128, NDC * 8))
    dram("bt", (128, D))
    dram("br", (8, 1))
    dram("cos", (128, KRANGE))
    dram("sin", (128, KRANGE))
    dram("ident", (128, 128))
    dram("ones", (128, 128))
    dram("rot", (128, 128))
    dram("e8", (8, 128))
    dram("identf", (128, 128))
    dram("aABf", (128, 512))
    dram("aABn", (128, 512))
    dram("maskone", (128, NKT * 128))
    t["d_out"] = nc.dram_tensor("out", [2, 128, NDC, HF], F32, kind="ExternalOutput").ap()
    t["d_dbg"] = nc.dram_tensor("dbg", [2, 128, KRANGE], BF16, kind="ExternalOutput").ap()
    t["d_dbg2"] = nc.dram_tensor("dbg2", [128, NDC * HF], BF16, kind="ExternalOutput").ap()
    t["d_dbg3"] = nc.dram_tensor("dbg3", [128, NDC * HF], BF16, kind="ExternalOutput").ap()
    return t


def _alloc(nc, t, stack):
    def sb(name, shape, dt):
        t[name] = stack.enter_context(nc.sbuf_tensor("sb_" + name, list(shape), dt))

    sb("ident_bf", (128, 128), BF16)
    sb("ones_bf", (128, 128), BF16)
    sb("rot_bf", (128, 128), BF16)
    sb("e8_bf", (8, 128), BF16)
    sb("ident_f", (128, 128), F32)
    sb("aABf", (128, 512), F32)
    sb("aABn", (128, 512), F32)
    sb("maskone_bf", (128, NKT * 128), BF16)
    sb("brt", (8, 1), F32)
    sb("at_bf", (128, NDC * 128), BF16)
    sb("wr_bf", (128, NDC * 8), BF16)
    sb("bt_bf", (128, D), BF16)
    sb("krT", (128, KRANGE), BF16)
    sb("vrow", (128, KRANGE), BF16)
    sb("xh", (128, NDC * HF), F32)
    sb("ya", (128, NDC * HF), BF16)
    sb("qb", (128, NDC * HF), BF16)
    sb("ws0", (128, 2048), BF16)
    sb("ws1", (128, 2048), BF16)
    sb("qk_raw", (128, 1024), BF16)
    sb("zbuf", (128, HF), BF16)
    sb("sig0", (128, 6 * 256), BF16)
    sb("sig1", (128, 6 * 256), BF16)
    sb("expb", (8, HF), BF16)
    sb("pool", (128, 7 * 512), F32)
    t["pbig"] = stack.enter_context(nc.psum_tensor("pbig", [128, 4096], F32))


def _host_inputs(L0_attn_x, L0_attn_Wq, L0_attn_Wk, L0_attn_Wv, L0_attn_Wo,
                 L0_ffn_A, L0_ffn_B, L0_ffn_Wr, L0_ffn_br):
    x = np.asarray(L0_attn_x, dtype=np.float32)[0]
    Wq = np.ascontiguousarray(np.asarray(L0_attn_Wq, np.float32))
    Wk = np.asarray(L0_attn_Wk, np.float32)
    Wv = np.asarray(L0_attn_Wv, np.float32)
    Wo = np.ascontiguousarray(np.asarray(L0_attn_Wo, np.float32))
    A = np.asarray(L0_ffn_A, np.float32)
    B = np.asarray(L0_ffn_B, np.float32)
    Wr = np.asarray(L0_ffn_Wr, np.float32)
    br = np.asarray(L0_ffn_br, np.float32)

    xT = np.ascontiguousarray(x.T)
    cos, sin = _rope_tables()
    cosT = np.ascontiguousarray(cos.T)
    sinT = np.ascontiguousarray(sin.T)

    wq_l = np.ascontiguousarray(Wq.reshape(NH, 128, NDC, 128).transpose(0, 3, 2, 1))
    wo_l = np.ascontiguousarray(Wo.reshape(NDC, 128, NDC, 128).transpose(0, 3, 2, 1))
    wk_l = np.ascontiguousarray(Wk.reshape(128, NDC, 128).transpose(2, 1, 0))
    wv_l = np.ascontiguousarray(Wv.reshape(128, NDC, 128).transpose(2, 1, 0))
    at_l = np.ascontiguousarray(A.reshape(128, NDC, 128).transpose(2, 1, 0)).reshape(128, NDC * 128)
    wr_l = np.ascontiguousarray(Wr.reshape(8, NDC, 128).transpose(2, 1, 0)).reshape(128, NDC * 8)
    bt_l = np.ascontiguousarray(B.transpose(1, 0, 2).reshape(D, 128).T)
    br_l = br.reshape(8, 1)

    ident = np.eye(128, dtype=np.float32)
    ones = np.ones((128, 128), dtype=np.float32)
    rot = np.zeros((128, 128), dtype=np.float32)
    for i in range(64):
        rot[i + 64, i] = -1.0
        rot[i, i + 64] = 1.0
    e8 = np.zeros((8, 128), dtype=np.float32)
    for hh in range(8):
        e8[hh, hh * 16:(hh + 1) * 16] = 8.0
    alpha = np.linspace(0.0, 1.0, 256, dtype=np.float32)
    aABn = np.zeros((128, 512), dtype=np.float32)
    aABn[:, 0:256] = 1.0 - alpha[None, :]
    aABn[:, 256:512] = alpha[None, :]

    shared = {
        "wq": wq_l, "wo": wo_l, "wk": wk_l, "wv": wv_l,
        "at": at_l, "wr": wr_l, "bt": bt_l, "br": br_l,
        "ident": ident, "identf": ident, "ones": ones, "rot": rot, "e8": e8,
        "aABn": aABn,
    }

    in_maps = []
    for c in range(NCORE):
        g0 = c * OWN - HALO
        xt_c = np.zeros((D, KRANGE), dtype=np.float32)
        lo, hi = max(0, g0), min(S, g0 + KRANGE)
        xt_c[:, lo - g0:hi - g0] = xT[:, lo:hi]
        xt_l = np.ascontiguousarray(
            xt_c.reshape(NDC, 128, 6, 256).transpose(2, 1, 0, 3))
        cos_c = np.zeros((DH, KRANGE), dtype=np.float32)
        sin_c = np.zeros((DH, KRANGE), dtype=np.float32)
        cos_c[:, lo - g0:hi - g0] = cosT[:, lo:hi]
        sin_c[:, lo - g0:hi - g0] = sinT[:, lo:hi]
        mo = np.zeros((128, NKT * 128), dtype=np.float32)
        for tt in range(NKT):
            grow = g0 + tt * 128 + np.arange(128)
            valid = ((grow >= 0) & (grow < S)).astype(np.float32)
            mo[:, tt * 128:(tt + 1) * 128] = valid[:, None]
        if c == 0:
            aABf = np.zeros((128, 512), dtype=np.float32)
            aABf[:, 256:512] = 1.0
        else:
            aABf = aABn
        m = dict(shared)
        m.update({"xt": xt_l, "cos": np.ascontiguousarray(cos_c),
                  "sin": np.ascontiguousarray(sin_c), "maskone": mo,
                  "aABf": np.ascontiguousarray(aABf)})
        in_maps.append(m)
    return in_maps


def kernel(L0_attn_x, L0_attn_Wq, L0_attn_Wk, L0_attn_Wv, L0_attn_Wo,
           L0_ffn_A, L0_ffn_B, L0_ffn_Wr, L0_ffn_br):
    from contextlib import ExitStack

    in_maps = _host_inputs(L0_attn_x, L0_attn_Wq, L0_attn_Wk, L0_attn_Wv,
                           L0_attn_Wo, L0_ffn_A, L0_ffn_B, L0_ffn_Wr, L0_ffn_br)

    nc = bass.Bass("TRN2", target_bir_lowering=False)
    t = _declare(nc)
    p = P()
    with ExitStack() as stack:
        _alloc(nc, t, stack)
        sems = {}
        for s in ("sW", "sX", "sO", "sPE", "sACT", "sDVE", "sPOOL"):
            sems[s] = stack.enter_context(nc.semaphore(s))
        p.sems = sems
        build_program(p, t)
        blk = stack.enter_context(nc.Block())
        for eng_name in P.ENGS:
            closures = p.q[eng_name]

            def make_body(closures):
                def body(eng):
                    for fn in closures:
                        fn(eng)
                return body
            getattr(blk, eng_name)(make_body(closures))

    res = run_bass_kernel_spmd(nc, in_maps, core_ids=list(range(NCORE)))
    outs = []
    for c in range(NCORE):
        o = res.results[c]["out"]
        outs.append(o.transpose(2, 1, 0, 3).reshape(D, OWN))
    full = np.concatenate(outs, axis=1)
    return np.ascontiguousarray(full.T).reshape(1, S, D).astype(np.float32)
